# version 03
# baseline (speedup 1.0000x reference)
import contextlib
import numpy as np
import ml_dtypes
import concourse.bass as bass
import concourse.mybir as mybir
from concourse.bass_utils import run_bass_kernel_spmd
from concourse.alu_op_type import AluOpType as Op

F32 = mybir.dt.float32
BF16 = mybir.dt.bfloat16
I32 = mybir.dt.int32
AF = mybir.ActivationFunctionType
AX = mybir.AxisListType.X
BFD = ml_dtypes.bfloat16

B, T, D = 4, 4096, 1024
H, NJ = 8, 32
HID0 = 2816
HID1 = 4096
STATE = 256
TOK = 2048
NTILE = TOK // 128
CORES = list(range(8))


def ein(nc_, name, shape, dtype=F32):
    return nc_.dram_tensor(name, list(shape), dtype, kind="ExternalInput").ap()


def eout(nc_, name, shape, dtype=F32):
    return nc_.dram_tensor(name, list(shape), dtype, kind="ExternalOutput").ap()



def emit_quant(v, x_ap, q_ap, scr, W):
    nb = W // 16
    hw, hb = W // 2, nb // 2
    A_, Bb, C_, D_, E_ = scr["A"], scr["B"], scr["C"], scr["D"], scr["E"]
    am, sc = scr["am"], scr["sc"]

    def big(buf, h, dt_):
        return buf[:, h * hw:(h + 1) * hw].bitcast(dt_)

    def small(buf, h):
        return buf[:, h * hb:(h + 1) * hb]

    def blk(ap):
        return ap.rearrange("p (g j) -> p g j", j=16)

    xh = [x_ap[:, h * hw:(h + 1) * hw] for h in range(2)]
    qh = [q_ap[:, h * hw:(h + 1) * hw] for h in range(2)]
    for h in range(2):
        v.reduce_max(small(am, h), blk(xh[h]), axis=AX, apply_absolute_value=True)
    for h in range(2):
        v.nop(cycle_cnt=64)
    for h in range(2):
        v.tensor_scalar(small(sc, h), small(am, h), 1.0 / 6.0, 1e-12, Op.mult, Op.max)
    for h in range(2):
        v.reciprocal(small(am, h), small(sc, h))
    for h in range(2):
        v.tensor_tensor(blk(big(A_, h, F32)), blk(xh[h]),
                        small(am, h).unsqueeze(2).broadcast_to([128, hb, 16]), Op.mult)
    for h in range(2):
        v.tensor_scalar(big(Bb, h, I32), big(A_, h, F32), 2.0, None, Op.mult)
    for h in range(2):
        v.tensor_copy(big(C_, h, I32), big(A_, h, F32))
    for h in range(2):
        v.tensor_scalar(big(D_, h, I32), big(A_, h, F32), 0.5, None, Op.mult)
    for h in range(2):
        v.tensor_scalar(big(E_, h, F32), big(Bb, h, I32), 4.0, -4.0, Op.min, Op.max)
    for h in range(2):
        v.tensor_scalar(big(A_, h, F32), big(C_, h, I32), 4.0, -4.0, Op.min, Op.max)
    for h in range(2):
        v.tensor_scalar(big(Bb, h, F32), big(C_, h, I32), 2.0, -2.0, Op.min, Op.max)
    for h in range(2):
        v.tensor_tensor(big(C_, h, F32), big(A_, h, F32), big(Bb, h, F32), Op.subtract)
    for h in range(2):
        v.tensor_scalar(big(A_, h, F32), big(D_, h, I32), 2.0, -2.0, Op.min, Op.max)
    for h in range(2):
        v.tensor_tensor(big(Bb, h, F32), big(D_, h, I32), big(A_, h, F32), Op.subtract)
    for h in range(2):
        v.scalar_tensor_tensor(big(D_, h, F32), big(Bb, h, F32), 2.0, big(C_, h, F32),
                               Op.mult, Op.add)
    for h in range(2):
        v.tensor_scalar(small(am, h), small(sc, h), 0.5, None, Op.mult)
    for h in range(2):
        v.scalar_tensor_tensor(big(A_, h, F32), big(D_, h, F32), 2.0, big(E_, h, F32),
                               Op.mult, Op.add)
    for h in range(2):
        v.tensor_tensor(blk(qh[h]), blk(big(A_, h, F32)),
                        small(am, h).unsqueeze(2).broadcast_to([128, hb, 16]), Op.mult)


def quant_scratch(es, nc_, W, tag, share_A=None):
    nb = W // 16
    d = {}
    for nm in "BCDE":
        d[nm] = es.enter_context(nc_.sbuf_tensor(f"q{tag}{nm}", [128, W], F32))
    if share_A is None:
        d["A"] = es.enter_context(nc_.sbuf_tensor(f"q{tag}A", [128, W], F32))
    else:
        d["A"] = share_A
    d["am"] = es.enter_context(nc_.sbuf_tensor(f"q{tag}am", [128, nb], F32))
    d["sc"] = es.enter_context(nc_.sbuf_tensor(f"q{tag}sc", [128, nb], F32))
    return d



def build_W():
    nc_ = bass.Bass("TRN2", target_bir_lowering=False)
    wg = ein(nc_, "wg", [352, 1024])
    wu = ein(nc_, "wu", [352, 1024])
    wd = ein(nc_, "wd", [128, HID0])
    wfc = ein(nc_, "wfc", [512, 1024])
    wpj = ein(nc_, "wpj", [128, HID1])
    wv = ein(nc_, "wv", [32, 1024])
    qg = eout(nc_, "qg", [352, 1024], BF16)
    qu = eout(nc_, "qu", [352, 1024], BF16)
    qd = eout(nc_, "qd", [128, HID0], BF16)
    efc = eout(nc_, "efc", [512, 1024], BF16)
    epj = eout(nc_, "epj", [128, HID1], BF16)
    nwv = eout(nc_, "nwv", [32, 1024])

    with contextlib.ExitStack() as es:
        sg = es.enter_context(nc_.sbuf_tensor("sg", [128, 3 * 1024], F32))
        su = es.enter_context(nc_.sbuf_tensor("su", [128, 3 * 1024], F32))
        sd = es.enter_context(nc_.sbuf_tensor("sd", [128, HID0], F32))
        sfc = es.enter_context(nc_.sbuf_tensor("sfc", [128, 4 * 1024], F32))
        spj = es.enter_context(nc_.sbuf_tensor("spj", [128, HID1], F32))
        svv = es.enter_context(nc_.sbuf_tensor("svv", [32, 1024], F32))
        og = es.enter_context(nc_.sbuf_tensor("og", [128, 3 * 1024], BF16))
        ou = es.enter_context(nc_.sbuf_tensor("ou", [128, 3 * 1024], BF16))
        od = es.enter_context(nc_.sbuf_tensor("od", [128, HID0], BF16))
        ofc = es.enter_context(nc_.sbuf_tensor("ofc", [128, 4 * 1024], BF16))
        opj = es.enter_context(nc_.sbuf_tensor("opj", [128, HID1], BF16))
        onv = es.enter_context(nc_.sbuf_tensor("onv", [32, 1024], F32))
        bab = es.enter_context(nc_.sbuf_tensor("bab", [128, HID1], F32))
        bsm = es.enter_context(nc_.sbuf_tensor("bsm", [128, 8], F32))
        bsc = es.enter_context(nc_.sbuf_tensor("bsc", [128, 8], F32))
        brs = es.enter_context(nc_.sbuf_tensor("brs", [128, 8], F32))
        btn = es.enter_context(nc_.sbuf_tensor("btn", [128, HID1], I32))
        btc = es.enter_context(nc_.sbuf_tensor("btc", [128, HID1], F32))
        qs = {
            "A": bab[:, 0:HID0],
            "B": btn[:, 0:HID0].bitcast(F32),
            "C": btc[:, 0:HID0],
            "D": es.enter_context(nc_.sbuf_tensor("qwD", [128, HID0], F32)),
            "E": es.enter_context(nc_.sbuf_tensor("qwE", [128, HID0], F32)),
            "am": es.enter_context(nc_.sbuf_tensor("qwam", [128, HID0 // 16], F32)),
            "sc": es.enter_context(nc_.sbuf_tensor("qwsc", [128, HID0 // 16], F32)),
        }
        ld = es.enter_context(nc_.semaphore("ld"))
        vs = es.enter_context(nc_.semaphore("vs"))
        st = es.enter_context(nc_.semaphore("st"))
        block = es.enter_context(nc_.Block())

        loads = [(sg[0:128, 0:1024], wg[0:128, :]), (sg[0:128, 1024:2048], wg[128:256, :]),
                 (sg[0:96, 2048:3072], wg[256:352, :]), (su[0:128, 0:1024], wu[0:128, :]),
                 (su[0:128, 1024:2048], wu[128:256, :]), (su[0:96, 2048:3072], wu[256:352, :]),
                 (sd[:], wd), (sfc[0:128, 0:1024], wfc[0:128, :]),
                 (sfc[0:128, 1024:2048], wfc[128:256, :]), (sfc[0:128, 2048:3072], wfc[256:384, :]),
                 (sfc[0:128, 3072:4096], wfc[384:512, :]), (spj[:], wpj), (svv[:], wv)]

        @block.sync
        def _(sync):
            for dst, src in loads:
                sync.dma_start(out=dst, in_=src).then_inc(ld, 16)

        @block.vector
        def _(v):
            v.wait_ge(ld, 16 * len(loads))
            for a, b_ in [(sg[:, 0:1024], og[:, 0:1024]), (sg[:, 1024:2048], og[:, 1024:2048]),
                          (sg[:, 2048:3072], og[:, 2048:3072]), (su[:, 0:1024], ou[:, 0:1024]),
                          (su[:, 1024:2048], ou[:, 1024:2048]), (su[:, 2048:3072], ou[:, 2048:3072]),
                          (sd[:], od[:])]:
                emit_quant(v, a, b_, qs, a.shape[1])
            for w_, o_, wid, col in [(sfc[:, 0:1024], ofc[:, 0:1024], 1024, 0),
                                     (sfc[:, 1024:2048], ofc[:, 1024:2048], 1024, 1),
                                     (sfc[:, 2048:3072], ofc[:, 2048:3072], 1024, 2),
                                     (sfc[:, 3072:4096], ofc[:, 3072:4096], 1024, 3),
                                     (spj[:], opj[:], HID1, 4)]:
                v.reduce_sum(bsm[:, col:col + 1], w_, axis=AX,
                             apply_absolute_value=True)
                v.drain()
                v.tensor_scalar(bsc[:, col:col + 1], bsm[:, col:col + 1], 1.0 / wid, 1e-5,
                                Op.mult, Op.max)
                v.drain()
                v.reciprocal(brs[:, col:col + 1], bsc[:, col:col + 1])
                v.drain()
                v.tensor_scalar(btn[:, 0:wid], w_, brs[:, col:col + 1], None, Op.mult)
                v.drain()
                v.tensor_scalar(btc[:, 0:wid], btn[:, 0:wid], 1.0, -1.0, Op.min, Op.max)
                v.drain()
                v.tensor_scalar(o_, btc[:, 0:wid], bsc[:, col:col + 1], None, Op.mult)
                v.drain()
            v.tensor_scalar(onv[:], svv[:], -1.0, None, Op.mult).then_inc(vs, 1)

        stores = [(qg[0:128, :], og[0:128, 0:1024]), (qg[128:256, :], og[0:128, 1024:2048]),
                  (qg[256:352, :], og[0:96, 2048:3072]), (qu[0:128, :], ou[0:128, 0:1024]),
                  (qu[128:256, :], ou[0:128, 1024:2048]), (qu[256:352, :], ou[0:96, 2048:3072]),
                  (qd, od[:]), (efc[0:128, :], ofc[0:128, 0:1024]),
                  (efc[128:256, :], ofc[0:128, 1024:2048]), (efc[256:384, :], ofc[0:128, 2048:3072]),
                  (efc[384:512, :], ofc[0:128, 3072:4096]), (epj, opj[:]), (nwv, onv[:])]

        @block.gpsimd
        def _(g):
            g.wait_ge(vs, 1)
            for dst, src in stores:
                g.dma_start(out=dst, in_=src).then_inc(st, 16)
            g.wait_ge(st, 16 * len(stores))
    return nc_



def build_A0():
    nc_ = bass.Bass("TRN2", target_bir_lowering=False)
    x = ein(nc_, "x", [TOK, D])
    xnb_o = eout(nc_, "xnb_o", [TOK, D], BF16)
    xq_o = eout(nc_, "xq_o", [TOK, D], BF16)

    with contextlib.ExitStack() as es:
        sx = es.enter_context(nc_.sbuf_tensor("sx", [128, NTILE * D], F32))
        sc1 = es.enter_context(nc_.sbuf_tensor("sc1", [128, NTILE], F32))
        sc2 = es.enter_context(nc_.sbuf_tensor("sc2", [128, NTILE], F32))
        mu = es.enter_context(nc_.sbuf_tensor("mu", [128, NTILE], F32))
        var = es.enter_context(nc_.sbuf_tensor("var", [128, NTILE], F32))
        rsd = es.enter_context(nc_.sbuf_tensor("rsd", [128, NTILE], F32))
        lns = es.enter_context(nc_.sbuf_tensor("lns", [128, D], F32))
        seps = es.enter_context(nc_.sbuf_tensor("seps", [128, 1], F32))
        sxnb = es.enter_context(nc_.sbuf_tensor("sxnb", [128, NTILE * D], BF16))
        sxq = es.enter_context(nc_.sbuf_tensor("sxq", [128, NTILE * D], BF16))
        qs = quant_scratch(es, nc_, D, "x")
        ld = es.enter_context(nc_.semaphore("ld"))
        vs = es.enter_context(nc_.semaphore("vs"))
        ss = es.enter_context(nc_.semaphore("ss"))
        st = es.enter_context(nc_.semaphore("st"))
        block = es.enter_context(nc_.Block())

        @block.sync
        def _(sync):
            for i in range(NTILE):
                sync.dma_start(out=sx[:, i * D:(i + 1) * D],
                               in_=x[i * 128:(i + 1) * 128, :]).then_inc(ld, 16)

        @block.vector
        def _(v):
            v.memset(seps[:], 1e-5)
            v.wait_ge(ld, 16 * NTILE)
            for i in range(NTILE):
                xt = sx[:, i * D:(i + 1) * D]
                v.reduce_sum(sc1[:, i:i + 1], xt, axis=AX)
                v.scalar_tensor_tensor(lns[:], xt, 1.0, xt, Op.mult, Op.mult,
                                       accum_out=sc2[:, i:i + 1])
            v.tensor_scalar(mu[:], sc1[:], 1.0 / D, None, Op.mult)
            v.tensor_scalar(var[:], sc2[:], 1.0 / D, None, Op.mult)
            v.drain()
            v.tensor_tensor(sc1[:], mu[:], mu[:], Op.mult)
            v.drain()
            v.tensor_tensor(var[:], var[:], sc1[:], Op.subtract).then_inc(vs, 1)
            v.wait_ge(ss, 1)
            v.reciprocal(rsd[:], var[:])
            v.drain()
            for i in range(NTILE):
                xt = sx[:, i * D:(i + 1) * D]
                v.tensor_scalar(xt, xt, mu[:, i:i + 1], rsd[:, i:i + 1],
                                Op.subtract, Op.mult)
            v.drain()
            for i in range(NTILE):
                v.tensor_copy(sxnb[:, i * D:(i + 1) * D], sx[:, i * D:(i + 1) * D])
            for i in range(NTILE):
                emit_quant(v, sx[:, i * D:(i + 1) * D], sxq[:, i * D:(i + 1) * D], qs, D)
            v.drain()
            v.drain().then_inc(vs, 1)

        @block.scalar
        def _(s):
            s.wait_ge(vs, 1)
            s.activation(var[:], var[:], AF.Sqrt, bias=seps[:]).then_inc(ss, 1)

        @block.gpsimd
        def _(g):
            g.wait_ge(vs, 2)
            for i in range(NTILE):
                g.dma_start(out=xnb_o[i * 128:(i + 1) * 128, :],
                            in_=sxnb[:, i * D:(i + 1) * D]).then_inc(st, 16)
                g.dma_start(out=xq_o[i * 128:(i + 1) * 128, :],
                            in_=sxq[:, i * D:(i + 1) * D]).then_inc(st, 16)
            g.wait_ge(st, 16 * 2 * NTILE)
    return nc_



def load_cast(es, nc_, sync_ops, cast_ops, name, hbm_ap, rows, cols, want_bf16):
    nch = (rows + 127) // 128
    src_dt = hbm_ap.dtype
    tiles = []
    if want_bf16 and src_dt == F32:
        raw = es.enter_context(nc_.sbuf_tensor(f"{name}_f", [128, nch * cols], F32))
        out = es.enter_context(nc_.sbuf_tensor(f"{name}_b", [128, nch * cols], BF16))
        for j in range(nch):
            r0, r1 = j * 128, min(rows, (j + 1) * 128)
            sync_ops.append((raw[0:r1 - r0, j * cols:j * cols + cols], hbm_ap[r0:r1, :]))
            cast_ops.append((out[0:r1 - r0, j * cols:j * cols + cols],
                             raw[0:r1 - r0, j * cols:j * cols + cols]))
            tiles.append(out[0:r1 - r0, j * cols:j * cols + cols])
    else:
        dt_ = hbm_ap.dtype
        out = es.enter_context(nc_.sbuf_tensor(f"{name}_s", [128, nch * cols], dt_))
        for j in range(nch):
            r0, r1 = j * 128, min(rows, (j + 1) * 128)
            sync_ops.append((out[0:r1 - r0, j * cols:j * cols + cols], hbm_ap[r0:r1, :]))
            tiles.append(out[0:r1 - r0, j * cols:j * cols + cols])
    return tiles



def build_A1():
    nc_ = bass.Bass("TRN2", target_bir_lowering=False)
    xnT = ein(nc_, "xnT", [D, TOK], BF16)
    w_inT = ein(nc_, "w_inT", [D, D])
    kvqbT = ein(nc_, "kvqbT", [D, 1024])
    bbeta = ein(nc_, "bbeta", [1, 256])
    kn_o = eout(nc_, "kn_o", [TOK, 256], BF16)
    q_o = eout(nc_, "q_o", [TOK, 256], BF16)
    nv_o = eout(nc_, "nv_o", [TOK, 256])
    bt_o = eout(nc_, "bt_o", [TOK, 256])

    with contextlib.ExitStack() as es:
        sync_ops, cast_ops = [], []
        xnT_t = load_cast(es, nc_, sync_ops, cast_ops, "xnT", xnT, D, TOK, False)
        win_t = load_cast(es, nc_, sync_ops, cast_ops, "win", w_inT, D, D, True)
        kvb_t = load_cast(es, nc_, sync_ops, cast_ops, "kvb", kvqbT, D, 1024, True)
        sbb = es.enter_context(nc_.sbuf_tensor("sbb", [1, 256], F32))
        sync_ops.append((sbb[:], bbeta))
        sone = es.enter_context(nc_.sbuf_tensor("sone", [1, 128], F32))
        sxpT = es.enter_context(nc_.sbuf_tensor("sxpT", [128, 8 * TOK], BF16))
        spr = es.enter_context(nc_.sbuf_tensor("spr", [128, 1024], F32))
        k2 = es.enter_context(nc_.sbuf_tensor("k2", [128, 256], F32))
        k2s = es.enter_context(nc_.sbuf_tensor("k2s", [128, 8], F32))
        nrm = es.enter_context(nc_.sbuf_tensor("nrm", [128, 8], F32))
        rn = es.enter_context(nc_.sbuf_tensor("rn", [128, 8], F32))
        oknb = es.enter_context(nc_.sbuf_tensor("oknb", [128, 2 * 256], BF16))
        oqb = es.enter_context(nc_.sbuf_tensor("oqb", [128, 2 * 256], BF16))
        onv = es.enter_context(nc_.sbuf_tensor("onv", [128, 2 * 256], F32))
        obt = es.enter_context(nc_.sbuf_tensor("obt", [128, 2 * 256], F32))
        pA = es.enter_context(nc_.psum_tensor("pA", [128, 512], F32))
        pB = es.enter_context(nc_.psum_tensor("pB", [128, 512], F32))
        pC = es.enter_context(nc_.psum_tensor("pC", [128, 512], F32))
        pD = es.enter_context(nc_.psum_tensor("pD", [128, 512], F32))
        ld = es.enter_context(nc_.semaphore("ld"))
        cs = es.enter_context(nc_.semaphore("cs"))
        vs = es.enter_context(nc_.semaphore("vs"))
        ss = es.enter_context(nc_.semaphore("ss"))
        ts = es.enter_context(nc_.semaphore("ts"))
        st = es.enter_context(nc_.semaphore("st"))
        block = es.enter_context(nc_.Block())

        @block.sync
        def _(sync):
            for dst, src in sync_ops:
                sync.dma_start(out=dst, in_=src).then_inc(ld, 16)

        @block.gpsimd
        def _(g):
            g.wait_ge(ld, 16 * len(sync_ops))
            for dst, src in cast_ops:
                g.dma_start(out=dst, in_=src).then_inc(cs, 16)
            for i in range(NTILE):
                g.wait_ge(vs, 2 + 3 * i + 3)
                sl = (i % 2) * 256
                g.dma_start(out=kn_o[i * 128:(i + 1) * 128, :], in_=oknb[:, sl:sl + 256]).then_inc(st, 16)
                g.dma_start(out=q_o[i * 128:(i + 1) * 128, :], in_=oqb[:, sl:sl + 256]).then_inc(st, 16)
                g.dma_start(out=nv_o[i * 128:(i + 1) * 128, :], in_=onv[:, sl:sl + 256]).then_inc(st, 16)
                g.dma_start(out=bt_o[i * 128:(i + 1) * 128, :], in_=obt[:, sl:sl + 256]).then_inc(st, 16)
            g.wait_ge(st, 16 * 4 * NTILE)

        NCAST = len(cast_ops)

        @block.tensor
        def _(t):
            t.wait_ge(ld, 16 * len(sync_ops))
            t.wait_ge(cs, 16 * NCAST)
            t.wait_ge(vs, 1)
            tcnt = 0
            for j in range(8):
                for bl in range(4):
                    ps = pA if (j * 4 + bl) % 2 == 0 else pB
                    if tcnt >= 2:
                        t.wait_ge(ss, tcnt - 1)
                    for kc in range(8):
                        mi = t.matmul(ps[:, 0:512],
                                      win_t[kc][:, j * 128:(j + 1) * 128],
                                      xnT_t[kc][:, bl * 512:(bl + 1) * 512],
                                      start=(kc == 0), stop=(kc == 7))
                    mi.then_inc(ts, 1)
                    tcnt += 1
            for i in range(NTILE):
                if i >= 2:
                    t.wait_ge(vs, 2 + 3 * (i - 2) + 1)
                t.wait_ge(ss, 32)
                for half in range(2):
                    ps = (pC if i % 2 == 0 else pA) if half == 0 else (pD if i % 2 == 0 else pB)
                    for kc in range(8):
                        mi = t.matmul(ps[:, 0:512],
                                      sxpT[:, kc * TOK + i * 128: kc * TOK + (i + 1) * 128],
                                      kvb_t[kc][:, half * 512:(half + 1) * 512],
                                      start=(kc == 0),
                                      stop=(kc == 7 and half == 0))
                    if half == 1:
                        mi = t.matmul(ps[:, 256:512], sone[:], sbb[:],
                                      start=False, stop=True)
                mi.then_inc(ts, 1)

        @block.scalar
        def _(s):
            for n in range(32):
                j, bl = divmod(n, 4)
                ps = pA if n % 2 == 0 else pB
                s.wait_ge(ts, n + 1)
                s.activation(sxpT[:, j * TOK + bl * 512: j * TOK + (bl + 1) * 512],
                             ps[:, 0:512], AF.Silu).then_inc(ss, 1)
            for i in range(NTILE):
                s.wait_ge(vs, 2 + 3 * i + 2)
                s.activation(nrm[:], k2s[:], AF.Sqrt)
                sl = (i % 2) * 256
                s.activation(obt[:, sl:sl + 256], spr[:, 768:1024], AF.Sigmoid).then_inc(ss, 2)

        @block.vector
        def _(v):
            v.memset(sone[:], 1.0)
            v.drain().then_inc(vs, 1)
            v.drain().then_inc(vs, 1)
            for i in range(NTILE):
                v.wait_ge(ts, 32 + i + 1)
                if i >= 2:
                    v.wait_ge(st, 16 * 4 * (i - 1))
                pc = pC if i % 2 == 0 else pA
                pd = pD if i % 2 == 0 else pB
                v.tensor_copy(spr[:, 0:512], pc[:, 0:512])
                v.tensor_copy(spr[:, 512:1024], pd[:, 0:512])
                v.drain().then_inc(vs, 1)
                sl = (i % 2) * 256
                v.tensor_tensor(k2[:], spr[:, 0:256], spr[:, 0:256], Op.mult)
                v.tensor_copy(onv[:, sl:sl + 256], spr[:, 512:768])
                v.reduce_sum(k2s[:], k2[:].rearrange("p (g j) -> p g j", j=NJ), axis=AX)
                v.tensor_copy(oqb[:, sl:sl + 256], spr[:, 256:512])
                v.drain().then_inc(vs, 1)
                v.wait_ge(ss, 32 + 2 * (i + 1))
                v.tensor_scalar(nrm[:], nrm[:], 1e-6, None, Op.add)
                v.drain()
                v.reciprocal(rn[:], nrm[:])
                v.drain()
                v.tensor_tensor(oknb[:, sl:sl + 256].rearrange("p (g j) -> p g j", j=NJ),
                                spr[:, 0:256].rearrange("p (g j) -> p g j", j=NJ),
                                rn[:].unsqueeze(2).broadcast_to([128, 8, NJ]), Op.mult)
                v.drain()
                v.drain().then_inc(vs, 1)
    return nc_



def build_A2():
    nc_ = bass.Bass("TRN2", target_bir_lowering=False)
    xqT = ein(nc_, "xqT", [D, TOK], BF16)
    wqgT = ein(nc_, "wqgT", [D, HID0], BF16)
    wquT = ein(nc_, "wquT", [D, HID0], BF16)
    hq_o = eout(nc_, "hq_o", [TOK, HID0], BF16)

    with contextlib.ExitStack() as es:
        sync_ops, cast_ops = [], []
        xq_t = load_cast(es, nc_, sync_ops, cast_ops, "xqT", xqT, D, TOK, False)
        wg_t = load_cast(es, nc_, sync_ops, cast_ops, "wg", wqgT, D, HID0, False)
        wu_t = load_cast(es, nc_, sync_ops, cast_ops, "wu", wquT, D, HID0, False)
        sh = es.enter_context(nc_.sbuf_tensor("sh", [128, HID0], F32))
        ssg = es.enter_context(nc_.sbuf_tensor("ssg", [128, HID0], F32))
        shq = es.enter_context(nc_.sbuf_tensor("shq", [128, 2 * HID0], BF16))
        qs = quant_scratch(es, nc_, HID0, "h", share_A=ssg[:, 0:HID0])
        pbank = [es.enter_context(nc_.psum_tensor(f"p{k}", [128, 512], F32)) for k in range(8)]
        ld = es.enter_context(nc_.semaphore("ld"))
        vs = es.enter_context(nc_.semaphore("vs"))
        ss = es.enter_context(nc_.semaphore("ss"))
        e2 = es.enter_context(nc_.semaphore("e2"))
        ts = es.enter_context(nc_.semaphore("ts"))
        st = es.enter_context(nc_.semaphore("st"))
        block = es.enter_context(nc_.Block())

        NBLK = 6
        widths = [512, 512, 512, 512, 512, 256]
        offs = np.cumsum([0] + widths).tolist()

        @block.sync
        def _(sync):
            for dst, src in sync_ops:
                sync.dma_start(out=dst, in_=src).then_inc(ld, 16)

        @block.tensor
        def _(t):
            t.wait_ge(ld, 16 * len(sync_ops))
            for i in range(NTILE):
                for blkk in range(NBLK):
                    n = i * NBLK + blkk
                    if n >= 3:
                        t.wait_ge(e2, n - 2)
                    w = widths[blkk]; o = offs[blkk]
                    pg = pbank[2 * (blkk % 3)]
                    pu = pbank[2 * (blkk % 3) + 1]
                    for kc in range(8):
                        t.matmul(pg[:, 0:w], xq_t[kc][:, i * 128:(i + 1) * 128],
                                 wg_t[kc][:, o:o + w], start=(kc == 0), stop=(kc == 7))
                    for kc in range(8):
                        mi = t.matmul(pu[:, 0:w], xq_t[kc][:, i * 128:(i + 1) * 128],
                                      wu_t[kc][:, o:o + w], start=(kc == 0), stop=(kc == 7))
                    mi.then_inc(ts, 1)

        @block.scalar
        def _(s):
            for i in range(NTILE):
                if i >= 1:
                    s.wait_ge(vs, i)
                for blkk in range(NBLK):
                    w = widths[blkk]; o = offs[blkk]
                    pg = pbank[2 * (blkk % 3)]
                    s.wait_ge(ts, i * NBLK + blkk + 1)
                    s.activation(ssg[:, o:o + w], pg[:, 0:w], AF.Silu).then_inc(ss, 1)

        @block.vector
        def _(v):
            for i in range(NTILE):
                for blkk in range(NBLK):
                    w = widths[blkk]; o = offs[blkk]
                    pu = pbank[2 * (blkk % 3) + 1]
                    v.wait_ge(ss, i * NBLK + blkk + 1)
                    v.tensor_tensor(sh[:, o:o + w], ssg[:, o:o + w], pu[:, 0:w],
                                    Op.mult).then_inc(e2, 1)
                if i >= 2:
                    v.wait_ge(st, 16 * (i - 1))
                v.drain()
                emit_quant(v, sh[:], shq[:, (i % 2) * HID0:(i % 2) * HID0 + HID0], qs, HID0)
                v.drain()
                v.drain().then_inc(vs, 1)

        @block.gpsimd
        def _(g):
            for i in range(NTILE):
                g.wait_ge(vs, i + 1)
                g.dma_start(out=hq_o[i * 128:(i + 1) * 128, :],
                            in_=shq[:, (i % 2) * HID0:(i % 2) * HID0 + HID0]).then_inc(st, 16)
            g.wait_ge(st, 16 * NTILE)
    return nc_


def build_dn(name, K, Nout, act="silu"):
    nc_ = bass.Bass("TRN2", target_bir_lowering=False)
    inT = ein(nc_, "inT", [K, TOK], BF16)
    wT = ein(nc_, "wT", [K, Nout], BF16)
    o = eout(nc_, "o", [TOK, Nout])

    nch = K // 128
    with contextlib.ExitStack() as es:
        sync_ops, cast_ops = [], []
        in_t = load_cast(es, nc_, sync_ops, cast_ops, "inT", inT, K, TOK, False)
        w_t = load_cast(es, nc_, sync_ops, cast_ops, "wT", wT, K, Nout, False)
        so = es.enter_context(nc_.sbuf_tensor("so", [128, 2 * Nout], F32))
        pb = [es.enter_context(nc_.psum_tensor(f"p{k}", [128, 512], F32)) for k in range(4)]
        ld = es.enter_context(nc_.semaphore("ld"))
        ss = es.enter_context(nc_.semaphore("ss"))
        ts = es.enter_context(nc_.semaphore("ts"))
        st = es.enter_context(nc_.semaphore("st"))
        block = es.enter_context(nc_.Block())
        NB = Nout // 512

        @block.sync
        def _(sync):
            for dst, src in sync_ops:
                sync.dma_start(out=dst, in_=src).then_inc(ld, 16)

        @block.tensor
        def _(t):
            t.wait_ge(ld, 16 * len(sync_ops))
            n = 0
            for i in range(NTILE):
                for blkk in range(NB):
                    ps = pb[n % 4]
                    if n >= 4:
                        t.wait_ge(ss, n - 3)
                    for kc in range(nch):
                        mi = t.matmul(ps[:, 0:512], in_t[kc][:, i * 128:(i + 1) * 128],
                                      w_t[kc][:, blkk * 512:(blkk + 1) * 512],
                                      start=(kc == 0), stop=(kc == nch - 1))
                    mi.then_inc(ts, 1)
                    n += 1

        @block.scalar
        def _(s):
            n = 0
            for i in range(NTILE):
                for blkk in range(NB):
                    ps = pb[n % 4]
                    s.wait_ge(ts, n + 1)
                    s.activation(so[:, (i % 2) * Nout + blkk * 512:(i % 2) * Nout + (blkk + 1) * 512],
                                 ps[:, 0:512], AF.Silu).then_inc(ss, 1)
                    n += 1
                if i >= 1:
                    s.wait_ge(st, 16 * i)

        @block.gpsimd
        def _(g):
            for i in range(NTILE):
                g.wait_ge(ss, NB * (i + 1))
                g.dma_start(out=o[i * 128:(i + 1) * 128, :],
                            in_=so[:, (i % 2) * Nout:(i % 2) * Nout + Nout]).then_inc(st, 16)
            g.wait_ge(st, 16 * NTILE)
    return nc_



def build_B2():
    nc_ = bass.Bass("TRN2", target_bir_lowering=False)
    KNB = ein(nc_, "KNB", [128, T * NJ], BF16)
    QB = ein(nc_, "QB", [128, T * NJ], BF16)
    NV = ein(nc_, "NV", [128, T])
    BT = ein(nc_, "BT", [128, T])
    OV = eout(nc_, "OV", [128, T])

    CH = 256
    NCH = T // CH
    with contextlib.ExitStack() as es:
        skn = es.enter_context(nc_.sbuf_tensor("skn", [128, 2 * CH * NJ], BF16))
        sqb = es.enter_context(nc_.sbuf_tensor("sqb", [128, 2 * CH * NJ], BF16))
        snv = es.enter_context(nc_.sbuf_tensor("snv", [128, T], F32))
        sbt = es.enter_context(nc_.sbuf_tensor("sbt", [128, T], F32))
        snr = es.enter_context(nc_.sbuf_tensor("snr", [128, T], F32))
        sS = es.enter_context(nc_.sbuf_tensor("sS", [128, 2 * NJ], F32))
        sP = es.enter_context(nc_.sbuf_tensor("sP", [128, NJ], F32))
        scr = es.enter_context(nc_.sbuf_tensor("scr", [128, NJ], F32))
        sw = es.enter_context(nc_.sbuf_tensor("sw", [128, 2], F32))
        ssq = es.enter_context(nc_.sbuf_tensor("ssq", [128, T], F32))
        sil = es.enter_context(nc_.sbuf_tensor("sil", [128, T], F32))
        sov = es.enter_context(nc_.sbuf_tensor("sov", [128, T], F32))
        ld = es.enter_context(nc_.semaphore("ld"))
        vs = es.enter_context(nc_.semaphore("vs"))
        ss = es.enter_context(nc_.semaphore("ss"))
        st = es.enter_context(nc_.semaphore("st"))
        block = es.enter_context(nc_.Block())

        @block.sync
        def _(sync):
            sync.dma_start(out=snv[:], in_=NV).then_inc(ld, 16)
            sync.dma_start(out=sbt[:], in_=BT).then_inc(ld, 16)
            for c in range(NCH):
                if c >= 2:
                    sync.wait_ge(vs, (c - 1) * CH + 1)
                sl = (c % 2) * CH * NJ
                sync.dma_start(out=skn[:, sl:sl + CH * NJ],
                               in_=KNB[:, c * CH * NJ:(c + 1) * CH * NJ]).then_inc(ld, 16)
                sync.dma_start(out=sqb[:, sl:sl + CH * NJ],
                               in_=QB[:, c * CH * NJ:(c + 1) * CH * NJ]).then_inc(ld, 16)

        @block.vector
        def _(v):
            v.wait_ge(ld, 16 * 2)
            v.memset(sS[:, 0:NJ], 0.0)
            v.reciprocal(snr[:], sbt[:])
            v.drain()
            v.tensor_scalar(snr[:], snr[:], -1.0, None, Op.mult)
            v.drain()
            for t_ in range(T):
                c, r = divmod(t_, CH)
                sl = (c % 2) * CH * NJ
                knb = skn[:, sl + r * NJ: sl + (r + 1) * NJ]
                if r == 0:
                    v.wait_ge(ld, 16 * (2 + 2 * (c + 1)))
                cur = sS[:, (t_ % 2) * NJ:(t_ % 2) * NJ + NJ]
                if t_ > 0:
                    v.wait_ge(ss, t_)
                v.scalar_tensor_tensor(scr[:], cur, 1.0, knb, Op.mult, Op.mult,
                                       accum_out=sw[:, 0:1])
                if t_ > 0:
                    pc, pr = divmod(t_ - 1, CH)
                    psl = (pc % 2) * CH * NJ
                    v.scalar_tensor_tensor(scr[:], cur, 1.0,
                                           sqb[:, psl + pr * NJ: psl + (pr + 1) * NJ],
                                           Op.mult, Op.mult,
                                           accum_out=ssq[:, t_ - 1:t_])
                else:
                    v.nop(cycle_cnt=64)
                v.tensor_scalar(sw[:, 1:2], sw[:, 0:1], snv[:, t_:t_ + 1],
                                snr[:, t_:t_ + 1], Op.add, Op.mult)
                v.drain()
                v.scalar_tensor_tensor(sP[:], knb, sw[:, 1:2], cur,
                                       Op.mult, Op.add).then_inc(vs, 1)
            v.wait_ge(ss, T)
            cl, rl = divmod(T - 1, CH)
            sll = (cl % 2) * CH * NJ
            v.scalar_tensor_tensor(scr[:], sS[:, (T % 2) * NJ:(T % 2) * NJ + NJ],
                                   1.0, sqb[:, sll + rl * NJ:sll + (rl + 1) * NJ],
                                   Op.mult, Op.mult, accum_out=ssq[:, T - 1:T])
            v.drain()
            v.drain().then_inc(vs, 1)
            v.wait_ge(ss, T + 1)
            v.tensor_tensor(sov[:], ssq[:], sil[:], Op.mult).then_inc(vs, 1)

        @block.scalar
        def _(s):
            for t_ in range(T):
                s.wait_ge(vs, t_ + 1)
                nxt = sS[:, ((t_ + 1) % 2) * NJ:((t_ + 1) % 2) * NJ + NJ]
                s.activation(nxt, sP[:], AF.Tanh, scale=sbt[:, t_:t_ + 1]).then_inc(ss, 1)
            s.wait_ge(vs, T + 1)
            s.activation(sil[:], ssq[:], AF.Silu).then_inc(ss, 1)

        @block.gpsimd
        def _(g):
            g.wait_ge(vs, T + 2)
            g.dma_start(out=OV, in_=sov[:]).then_inc(st, 16)
            g.wait_ge(st, 16)
    return nc_



def build_C1():
    nc_ = bass.Bass("TRN2", target_bir_lowering=False)
    x = ein(nc_, "x", [TOK, D])
    g0 = ein(nc_, "g0", [TOK, D])
    ovT = ein(nc_, "ovT", [256, TOK])
    w_outT = ein(nc_, "w_outT", [256, D])
    x1_o = eout(nc_, "x1_o", [TOK, D])
    xn1_o = eout(nc_, "xn1_o", [TOK, D], BF16)

    with contextlib.ExitStack() as es:
        sync_ops, cast_ops = [], []
        ov_t = load_cast(es, nc_, sync_ops, cast_ops, "ovT", ovT, 256, TOK, True)
        wo_t = load_cast(es, nc_, sync_ops, cast_ops, "woT", w_outT, 256, D, True)
        sx = es.enter_context(nc_.sbuf_tensor("sx", [128, NTILE * D], F32))
        sg0 = es.enter_context(nc_.sbuf_tensor("sg0", [128, 2 * D], F32))
        sy0 = es.enter_context(nc_.sbuf_tensor("sy0", [128, D], F32))
        sxnb = es.enter_context(nc_.sbuf_tensor("sxnb", [128, 2 * D], BF16))
        sc1 = es.enter_context(nc_.sbuf_tensor("sc1", [128, NTILE], F32))
        sc2 = es.enter_context(nc_.sbuf_tensor("sc2", [128, NTILE], F32))
        mu = es.enter_context(nc_.sbuf_tensor("mu", [128, NTILE], F32))
        var = es.enter_context(nc_.sbuf_tensor("var", [128, NTILE], F32))
        rsd = es.enter_context(nc_.sbuf_tensor("rsd", [128, NTILE], F32))
        lns = es.enter_context(nc_.sbuf_tensor("lns", [128, D], F32))
        seps = es.enter_context(nc_.sbuf_tensor("seps", [128, 1], F32))
        pb = [es.enter_context(nc_.psum_tensor(f"p{k}", [128, 512], F32)) for k in range(4)]
        ld = es.enter_context(nc_.semaphore("ld"))
        cs = es.enter_context(nc_.semaphore("cs"))
        vs = es.enter_context(nc_.semaphore("vs"))
        ss = es.enter_context(nc_.semaphore("ss"))
        e2 = es.enter_context(nc_.semaphore("e2"))
        ts = es.enter_context(nc_.semaphore("ts"))
        st = es.enter_context(nc_.semaphore("st"))
        block = es.enter_context(nc_.Block())

        @block.sync
        def _(sync):
            for dst, src in sync_ops:
                sync.dma_start(out=dst, in_=src).then_inc(ld, 16)
            for i in range(NTILE):
                sync.dma_start(out=sx[:, i * D:(i + 1) * D],
                               in_=x[i * 128:(i + 1) * 128, :]).then_inc(ld, 16)
            for i in range(NTILE):
                if i >= 2:
                    sync.wait_ge(e2, i - 1)
                sync.dma_start(out=sg0[:, (i % 2) * D:(i % 2) * D + D],
                               in_=g0[i * 128:(i + 1) * 128, :]).then_inc(ld, 16)

        NLD = len(sync_ops) + 2 * NTILE
        NCST = len(cast_ops)

        @block.gpsimd
        def _(g):
            g.wait_ge(ld, 16 * len(sync_ops))
            for dst, src in cast_ops:
                g.dma_start(out=dst, in_=src).then_inc(cs, 16)
            for i in range(NTILE):
                g.wait_ge(vs, i + 1)
                g.dma_start(out=x1_o[i * 128:(i + 1) * 128, :],
                            in_=sx[:, i * D:(i + 1) * D]).then_inc(st, 16)
                g.dma_start(out=xn1_o[i * 128:(i + 1) * 128, :],
                            in_=sxnb[:, 0:D] if False else sxnb[:, (i % 2) * D:(i % 2) * D + D]).then_inc(st, 16)
            g.wait_ge(st, 16 * 2 * NTILE)

        @block.tensor
        def _(t):
            t.wait_ge(cs, 16 * NCST)
            for i in range(NTILE):
                if i >= 2:
                    t.wait_ge(e2, i - 1)
                for half in range(2):
                    ps = pb[(2 * i + half) % 4]
                    for kc in range(2):
                        mi = t.matmul(ps[:, 0:512], ov_t[kc][:, i * 128:(i + 1) * 128],
                                      wo_t[kc][:, half * 512:(half + 1) * 512],
                                      start=(kc == 0), stop=(kc == 1))
                mi.then_inc(ts, 1)

        @block.vector
        def _(v):
            v.memset(seps[:], 1e-5)
            for i in range(NTILE):
                v.wait_ge(ts, i + 1)
                v.wait_ge(ld, 16 * (len(sync_ops) + NTILE + i + 1))
                xt = sx[:, i * D:(i + 1) * D]
                gt = sg0[:, (i % 2) * D:(i % 2) * D + D]
                v.tensor_copy(sy0[:, 0:512], pb[(2 * i) % 4][:, 0:512])
                v.tensor_copy(sy0[:, 512:1024], pb[(2 * i + 1) % 4][:, 0:512])
                v.drain()
                v.tensor_tensor(sy0[:], sy0[:], gt, Op.mult)
                v.drain()
                v.tensor_tensor(xt, xt, sy0[:], Op.add)
                v.drain()
                v.reduce_sum(sc1[:, i:i + 1], xt, axis=AX)
                v.scalar_tensor_tensor(lns[:], xt, 1.0, xt, Op.mult, Op.mult,
                                       accum_out=sc2[:, i:i + 1])
                v.drain().then_inc(e2, 1)
            v.tensor_scalar(mu[:], sc1[:], 1.0 / D, None, Op.mult)
            v.tensor_scalar(var[:], sc2[:], 1.0 / D, None, Op.mult)
            v.drain()
            v.tensor_tensor(sc1[:], mu[:], mu[:], Op.mult)
            v.drain()
            v.tensor_tensor(var[:], var[:], sc1[:], Op.subtract).then_inc(ss, 1)
            v.wait_ge(ss, 2)
            v.reciprocal(rsd[:], var[:])
            v.drain()
            for i in range(NTILE):
                if i >= 2:
                    v.wait_ge(st, 16 * 2 * (i - 1))
                xt = sx[:, i * D:(i + 1) * D]
                v.tensor_scalar(sxnb[:, (i % 2) * D:(i % 2) * D + D], xt,
                                mu[:, i:i + 1], rsd[:, i:i + 1], Op.subtract, Op.mult)
                v.drain()
                v.drain().then_inc(vs, 1)

        @block.scalar
        def _(s):
            s.wait_ge(ss, 1)
            s.activation(var[:], var[:], AF.Sqrt, bias=seps[:]).then_inc(ss, 1)
    return nc_



def build_C2():
    nc_ = bass.Bass("TRN2", target_bir_lowering=False)
    xn1T = ein(nc_, "xn1T", [D, TOK], BF16)
    bwT = ein(nc_, "bwT", [D, 256])
    bb = ein(nc_, "bb", [1, 256])
    dwT = ein(nc_, "dwT", [D, D])
    db = ein(nc_, "db", [1, D])
    vT = ein(nc_, "vT", [D, 16])
    bu_o = eout(nc_, "bu_o", [TOK, 256])
    du_o = eout(nc_, "du_o", [TOK, D])
    t1_o = eout(nc_, "t1_o", [TOK, 16])

    with contextlib.ExitStack() as es:
        sync_ops, cast_ops = [], []
        xn_t = load_cast(es, nc_, sync_ops, cast_ops, "xnT", xn1T, D, TOK, False)
        bw_t = load_cast(es, nc_, sync_ops, cast_ops, "bwT", bwT, D, 256, True)
        dw_t = load_cast(es, nc_, sync_ops, cast_ops, "dwT", dwT, D, D, True)
        v_t = load_cast(es, nc_, sync_ops, cast_ops, "vT", vT, D, 16, True)
        sbb = es.enter_context(nc_.sbuf_tensor("sbb", [1, 256], F32))
        sdb = es.enter_context(nc_.sbuf_tensor("sdb", [1, D], F32))
        sync_ops.append((sbb[:], bb))
        sync_ops.append((sdb[:], db))
        sone = es.enter_context(nc_.sbuf_tensor("sone", [1, 128], F32))
        sbu = es.enter_context(nc_.sbuf_tensor("sbu", [128, 2 * 256], F32))
        sdu = es.enter_context(nc_.sbuf_tensor("sdu", [128, 2 * D], F32))
        st1 = es.enter_context(nc_.sbuf_tensor("st1", [128, 2 * 16], F32))
        pb = [es.enter_context(nc_.psum_tensor(f"p{k}", [128, 512], F32)) for k in range(6)]
        ld = es.enter_context(nc_.semaphore("ld"))
        cs = es.enter_context(nc_.semaphore("cs"))
        vs = es.enter_context(nc_.semaphore("vs"))
        ts = es.enter_context(nc_.semaphore("ts"))
        st = es.enter_context(nc_.semaphore("st"))
        block = es.enter_context(nc_.Block())

        @block.sync
        def _(sync):
            for dst, src in sync_ops:
                sync.dma_start(out=dst, in_=src).then_inc(ld, 16)

        @block.gpsimd
        def _(g):
            g.wait_ge(ld, 16 * len(sync_ops))
            for dst, src in cast_ops:
                g.dma_start(out=dst, in_=src).then_inc(cs, 16)
            for i in range(NTILE):
                g.wait_ge(vs, i + 1)
                sl = i % 2
                g.dma_start(out=bu_o[i * 128:(i + 1) * 128, :],
                            in_=sbu[:, sl * 256:sl * 256 + 256]).then_inc(st, 16)
                g.dma_start(out=du_o[i * 128:(i + 1) * 128, :],
                            in_=sdu[:, sl * D:sl * D + D]).then_inc(st, 16)
                g.dma_start(out=t1_o[i * 128:(i + 1) * 128, :],
                            in_=st1[:, sl * 16:sl * 16 + 16]).then_inc(st, 16)
            g.wait_ge(st, 16 * 3 * NTILE)

        @block.vector
        def _(v):
            v.memset(sone[:], 1.0)
            v.sem_inc(vs, 0) if False else None
            v.drain()
            for i in range(NTILE):
                v.wait_ge(ts, 4 * i + 4)
                if i >= 2:
                    v.wait_ge(st, 16 * 3 * (i - 1))
                sl = i % 2
                v.tensor_copy(sbu[:, sl * 256:sl * 256 + 256], pb[(4 * i) % 6][:, 0:256])
                v.tensor_copy(sdu[:, sl * D:sl * D + 512], pb[(4 * i + 1) % 6][:, 0:512])
                v.tensor_copy(sdu[:, sl * D + 512:sl * D + 1024], pb[(4 * i + 2) % 6][:, 0:512])
                v.tensor_copy(st1[:, sl * 16:sl * 16 + 16], pb[(4 * i + 3) % 6][:, 0:16])
                v.drain()
                v.drain().then_inc(vs, 1)

        @block.tensor
        def _(t):
            t.wait_ge(cs, 16 * len(cast_ops))
            t.wait_ge(ld, 16 * len(sync_ops))
            t.wait_ge(vs, 0)
            for i in range(NTILE):
                if i >= 1:
                    t.wait_ge(vs, i)
                stat = [xn_t[kc][:, i * 128:(i + 1) * 128] for kc in range(8)]
                ps = pb[(4 * i) % 6]
                for kc in range(8):
                    t.matmul(ps[:, 0:256], stat[kc], bw_t[kc][:, 0:256],
                             start=(kc == 0), stop=False)
                t.matmul(ps[:, 0:256], sone[:], sbb[:],
                         start=False, stop=True).then_inc(ts, 1)
                for half in range(2):
                    ps2 = pb[(4 * i + 1 + half) % 6]
                    for kc in range(8):
                        t.matmul(ps2[:, 0:512], stat[kc],
                                 dw_t[kc][:, half * 512:(half + 1) * 512],
                                 start=(kc == 0), stop=False)
                    t.matmul(ps2[:, 0:512], sone[:], sdb[0:1, half * 512:(half + 1) * 512],
                             start=False, stop=True).then_inc(ts, 1)
                ps3 = pb[(4 * i + 3) % 6]
                for kc in range(8):
                    mi = t.matmul(ps3[:, 0:16], stat[kc], v_t[kc][:, 0:16],
                                  start=(kc == 0), stop=(kc == 7))
                mi.then_inc(ts, 1)
    return nc_



def build_C3():
    nc_ = bass.Bass("TRN2", target_bir_lowering=False)
    xn1T = ein(nc_, "xn1T", [D, TOK], BF16)
    t1T = ein(nc_, "t1T", [16, TOK])
    fbT = ein(nc_, "fbT", [16, 16])
    uT = ein(nc_, "uT", [16, HID1])
    wfcT = ein(nc_, "wfcT", [D, HID1], BF16)
    gate = ein(nc_, "gate", [1, 1])
    r2_o = eout(nc_, "r2_o", [TOK, HID1], BF16)

    with contextlib.ExitStack() as es:
        sync_ops, cast_ops = [], []
        xn_t = load_cast(es, nc_, sync_ops, cast_ops, "xnT", xn1T, D, TOK, False)
        t1_t = load_cast(es, nc_, sync_ops, cast_ops, "t1T", t1T, 16, TOK, True)
        fb_t = load_cast(es, nc_, sync_ops, cast_ops, "fbT", fbT, 16, 16, True)
        u_t = load_cast(es, nc_, sync_ops, cast_ops, "uT", uT, 16, HID1, True)
        wf_t = load_cast(es, nc_, sync_ops, cast_ops, "wfcT", wfcT, D, HID1, False)
        sgt = es.enter_context(nc_.sbuf_tensor("sgt", [1, 1], F32))
        sync_ops.append((sgt[:], gate))
        st2 = es.enter_context(nc_.sbuf_tensor("st2", [16, TOK], BF16))
        st2f = es.enter_context(nc_.sbuf_tensor("st2f", [16, TOK], F32))
        sgb = es.enter_context(nc_.sbuf_tensor("sgb", [16, 1], F32))
        sre = es.enter_context(nc_.sbuf_tensor("sre", [128, HID1], F32))
        sr2 = es.enter_context(nc_.sbuf_tensor("sr2", [128, 2 * HID1], BF16))
        pb = [es.enter_context(nc_.psum_tensor(f"p{k}", [128, 512], F32)) for k in range(4)]
        pt2 = es.enter_context(nc_.psum_tensor("pt2", [16, 512], F32))
        ld = es.enter_context(nc_.semaphore("ld"))
        cs = es.enter_context(nc_.semaphore("cs"))
        vs = es.enter_context(nc_.semaphore("vs"))
        ss = es.enter_context(nc_.semaphore("ss"))
        e2 = es.enter_context(nc_.semaphore("e2"))
        e3 = es.enter_context(nc_.semaphore("e3"))
        ts = es.enter_context(nc_.semaphore("ts"))
        st = es.enter_context(nc_.semaphore("st"))
        block = es.enter_context(nc_.Block())

        NSYNC = len(sync_ops)

        @block.sync
        def _(sync):
            for dst, src in sync_ops:
                sync.dma_start(out=dst, in_=src).then_inc(ld, 16)
            sync.wait_ge(ld, 16 * NSYNC)
            sync.dma_start(out=sgb[:], in_=gate[0:1, 0:1].broadcast_to([16, 1])).then_inc(ld, 16)

        @block.gpsimd
        def _(g):
            g.wait_ge(ld, 16 * (len(sync_ops) + 1))
            for dst, src in cast_ops:
                g.dma_start(out=dst, in_=src).then_inc(cs, 16)
            for i in range(NTILE):
                g.wait_ge(vs, i + 2)
                g.dma_start(out=r2_o[i * 128:(i + 1) * 128, :],
                            in_=sr2[:, (i % 2) * HID1:(i % 2) * HID1 + HID1]).then_inc(st, 16)
            g.wait_ge(st, 16 * NTILE)

        @block.tensor
        def _(t):
            t.wait_ge(cs, 16 * len(cast_ops))
            for bl in range(4):
                if bl >= 1:
                    t.wait_ge(e2, bl)
                t.matmul(pt2[0:16, 0:512], fb_t[0][0:16, 0:16],
                         t1_t[0][0:16, bl * 512:(bl + 1) * 512],
                         start=True, stop=True).then_inc(ts, 1)
            t.wait_ge(vs, 1)
            for i in range(NTILE):
                if i >= 1:
                    t.wait_ge(ss, 8 * i)
                for blkk in range(8):
                    ps = pb[(8 * i + blkk) % 4]
                    if 8 * i + blkk >= 4:
                        t.wait_ge(ss, 8 * i + blkk - 3)
                    for kc in range(8):
                        t.matmul(ps[:, 0:512], xn_t[kc][:, i * 128:(i + 1) * 128],
                                 wf_t[kc][:, blkk * 512:(blkk + 1) * 512],
                                 start=(kc == 0), stop=False)
                    t.matmul(ps[:, 0:512], st2[0:16, i * 128:(i + 1) * 128],
                             u_t[0][0:16, blkk * 512:(blkk + 1) * 512],
                             start=False, stop=True).then_inc(ts, 1)

        @block.vector
        def _(v):
            v.wait_ge(ld, 16 * (NSYNC + 1))
            for bl in range(4):
                v.wait_ge(ts, bl + 1)
                v.tensor_copy(st2f[0:16, bl * 512:(bl + 1) * 512],
                              pt2[0:16, 0:512]).then_inc(e2, 1)
            v.drain()
            v.tensor_scalar(st2[0:16, :], st2f[0:16, :], sgb[0:16, 0:1], None, Op.mult)
            v.drain()
            v.drain().then_inc(vs, 1)
            for i in range(NTILE):
                if i >= 2:
                    v.wait_ge(st, 16 * (i - 1))
                for blkk in range(8):
                    v.wait_ge(ss, 8 * i + blkk + 1)
                    o = blkk * 512
                    v.tensor_tensor(sr2[:, (i % 2) * HID1 + o:(i % 2) * HID1 + o + 512],
                                    sre[:, o:o + 512], sre[:, o:o + 512],
                                    Op.mult).then_inc(e3, 1)
                v.drain().then_inc(vs, 1)

        @block.scalar
        def _(s):
            for i in range(NTILE):
                for blkk in range(8):
                    n = 8 * i + blkk
                    ps = pb[n % 4]
                    s.wait_ge(ts, 4 + n + 1)
                    if n >= 8:
                        s.wait_ge(e3, n - 7)
                    s.activation(sre[:, blkk * 512:(blkk + 1) * 512], ps[:, 0:512],
                                 AF.Relu).then_inc(ss, 1)
    return nc_



def build_D():
    nc_ = bass.Bass("TRN2", target_bir_lowering=False)
    BU = ein(nc_, "BU", [128, T])
    Av = ein(nc_, "Av", [128, 1])
    SO = eout(nc_, "SO", [128, T])

    with contextlib.ExitStack() as es:
        sbu = es.enter_context(nc_.sbuf_tensor("sbu", [128, T], F32))
        sae = es.enter_context(nc_.sbuf_tensor("sae", [128, 1], F32))
        sst = es.enter_context(nc_.sbuf_tensor("sst", [128, T], F32))
        ph = [es.enter_context(nc_.psum_tensor(f"ph{k}", [128, 512], F32)) for k in range(8)]
        ld = es.enter_context(nc_.semaphore("ld"))
        ss = es.enter_context(nc_.semaphore("ss"))
        vs = es.enter_context(nc_.semaphore("vs"))
        st = es.enter_context(nc_.semaphore("st"))
        block = es.enter_context(nc_.Block())

        @block.sync
        def _(sync):
            sync.dma_start(out=sbu[:], in_=BU).then_inc(ld, 16)
            sync.dma_start(out=sae[:], in_=Av).then_inc(ld, 16)

        @block.scalar
        def _(s):
            s.wait_ge(ld, 32)
            s.activation(sae[:], sae[:], AF.Exp)
            s.memzero(ph[0][:, 0:1])
            for t_ in range(T):
                bk, colk = divmod(t_, 512)
                if t_ == 0:
                    s.activation(ph[0][:, 0:1], ph[0][:, 0:1], AF.Tanh,
                                 scale=sae[:], bias=sbu[:, 0:1])
                else:
                    pbk, pcol = divmod(t_ - 1, 512)
                    s.activation(ph[bk][:, colk:colk + 1], ph[pbk][:, pcol:pcol + 1],
                                 AF.Tanh, scale=sae[:], bias=sbu[:, t_:t_ + 1])
            s.nop().then_inc(ss, 1)

        @block.vector
        def _(v):
            v.wait_ge(ss, 1)
            for bk in range(8):
                v.tensor_copy(sst[:, bk * 512:(bk + 1) * 512], ph[bk][:, 0:512])
            v.drain()
            v.drain().then_inc(vs, 1)

        @block.gpsimd
        def _(g):
            g.wait_ge(vs, 1)
            g.dma_start(out=SO, in_=sst[:]).then_inc(st, 16)
            g.wait_ge(st, 16)
    return nc_



def build_E():
    nc_ = bass.Bass("TRN2", target_bir_lowering=False)
    x1 = ein(nc_, "x1", [TOK, D])
    du = ein(nc_, "du", [TOK, D])
    g1 = ein(nc_, "g1", [TOK, D])
    stT = ein(nc_, "stT", [256, TOK])
    cwT = ein(nc_, "cwT", [256, D])
    cb = ein(nc_, "cb", [1, D])
    o = eout(nc_, "o", [TOK, D])

    with contextlib.ExitStack() as es:
        sync_ops, cast_ops = [], []
        st_t = load_cast(es, nc_, sync_ops, cast_ops, "stT", stT, 256, TOK, True)
        cw_t = load_cast(es, nc_, sync_ops, cast_ops, "cwT", cwT, 256, D, True)
        scb = es.enter_context(nc_.sbuf_tensor("scb", [1, D], F32))
        sync_ops.append((scb[:], cb))
        sone = es.enter_context(nc_.sbuf_tensor("sone", [1, 128], F32))
        sx1 = es.enter_context(nc_.sbuf_tensor("sx1", [128, 2 * D], F32))
        sdu = es.enter_context(nc_.sbuf_tensor("sdu", [128, 2 * D], F32))
        sg1 = es.enter_context(nc_.sbuf_tensor("sg1", [128, 2 * D], F32))
        sy1 = es.enter_context(nc_.sbuf_tensor("sy1", [128, D], F32))
        soo = es.enter_context(nc_.sbuf_tensor("soo", [128, 2 * D], F32))
        pb = [es.enter_context(nc_.psum_tensor(f"p{k}", [128, 512], F32)) for k in range(4)]
        ld = es.enter_context(nc_.semaphore("ld"))
        cs = es.enter_context(nc_.semaphore("cs"))
        vs = es.enter_context(nc_.semaphore("vs"))
        ts = es.enter_context(nc_.semaphore("ts"))
        st = es.enter_context(nc_.semaphore("st"))
        block = es.enter_context(nc_.Block())

        @block.sync
        def _(sync):
            for dst, src in sync_ops:
                sync.dma_start(out=dst, in_=src).then_inc(ld, 16)
            for i in range(NTILE):
                if i >= 2:
                    sync.wait_ge(vs, i - 1)
                sl = (i % 2) * D
                sync.dma_start(out=sx1[:, sl:sl + D], in_=x1[i * 128:(i + 1) * 128, :]).then_inc(ld, 16)
                sync.dma_start(out=sdu[:, sl:sl + D], in_=du[i * 128:(i + 1) * 128, :]).then_inc(ld, 16)
                sync.dma_start(out=sg1[:, sl:sl + D], in_=g1[i * 128:(i + 1) * 128, :]).then_inc(ld, 16)


        NW = len(sync_ops)

        @block.gpsimd
        def _(g):
            g.wait_ge(ld, 16 * NW)
            for dst, src in cast_ops:
                g.dma_start(out=dst, in_=src).then_inc(cs, 16)
            for i in range(NTILE):
                g.wait_ge(vs, i + 1)
                g.dma_start(out=o[i * 128:(i + 1) * 128, :],
                            in_=soo[:, (i % 2) * D:(i % 2) * D + D]).then_inc(st, 16)
            g.wait_ge(st, 16 * NTILE)

        @block.vector
        def _(v):
            v.memset(sone[:], 1.0)
            v.drain()
            for i in range(NTILE):
                v.wait_ge(ts, i + 1)
                v.wait_ge(ld, 16 * (NW + 3 * (i + 1)))
                if i >= 2:
                    v.wait_ge(st, 16 * (i - 1))
                sl = (i % 2) * D
                v.tensor_copy(sy1[:, 0:512], pb[(2 * i) % 4][:, 0:512])
                v.tensor_copy(sy1[:, 512:1024], pb[(2 * i + 1) % 4][:, 0:512])
                v.drain()
                v.tensor_tensor(sy1[:], sy1[:], sdu[:, sl:sl + D], Op.add)
                v.drain()
                v.tensor_tensor(sy1[:], sy1[:], sg1[:, sl:sl + D], Op.mult)
                v.drain()
                v.tensor_tensor(soo[:, sl:sl + D], sy1[:], sx1[:, sl:sl + D], Op.add)
                v.drain()
                v.drain().then_inc(vs, 1)

        @block.tensor
        def _(t):
            t.wait_ge(cs, 16 * len(cast_ops))
            t.wait_ge(ld, 16 * NW)
            for i in range(NTILE):
                if i >= 2:
                    t.wait_ge(vs, i - 1)
                for half in range(2):
                    ps = pb[(2 * i + half) % 4]
                    for kc in range(2):
                        t.matmul(ps[:, 0:512], st_t[kc][:, i * 128:(i + 1) * 128],
                                 cw_t[kc][:, half * 512:(half + 1) * 512],
                                 start=(kc == 0), stop=False)
                    mi = t.matmul(ps[:, 0:512], sone[:], scb[0:1, half * 512:(half + 1) * 512],
                                  start=False, stop=True)
                mi.then_inc(ts, 1)
    return nc_



GROUPS = [[0, 1], [2, 3], [4, 5], [6, 7]]
CH = 128
NCHK = T // CH


class Sems:
    def __init__(self):
        self.c = {}

    def inc(self, name, by=1):
        self.c[name] = self.c.get(name, 0) + by
        return self.c[name]

    def get(self, name):
        return self.c.get(name, 0)


def build_mega():
    nc_ = bass.Bass("TRN2", target_bir_lowering=False)
    x_i = ein(nc_, "x", [TOK, D])
    winT_i = ein(nc_, "winT", [D, D])
    kvqbT_i = ein(nc_, "kvqbT", [D, 1024])
    bbeta_i = ein(nc_, "bbeta", [128, 2])
    gsel_i = nc_.dram_tensor("gsel", [1, 1], I32, kind="ExternalInput").ap()
    wg_i = ein(nc_, "wg", [352, 1024])
    wu_i = ein(nc_, "wu", [352, 1024])
    wd_i = ein(nc_, "wd", [128, HID0])
    wfc_i = ein(nc_, "wfc", [512, 1024])
    wpj_i = ein(nc_, "wpj", [128, HID1])
    qg_o = eout(nc_, "qg_o", [352, 1024], BF16)
    qu_o = eout(nc_, "qu_o", [352, 1024], BF16)
    qd_o = eout(nc_, "qd_o", [128, HID0], BF16)
    efc_o = eout(nc_, "efc_o", [512, 1024], BF16)
    epj_o = eout(nc_, "epj_o", [128, HID1], BF16)
    xqT_o = eout(nc_, "xqT_o", [D, TOK], BF16)
    xnT_o = eout(nc_, "xnT_o", [D, TOK], BF16)
    sov_o = eout(nc_, "sov_o", [128, T])
    cc1b_in = nc_.dram_tensor("cc1b_in", [TOK, 520], BF16, kind="Internal").ap()
    cc1b_out = nc_.dram_tensor("cc1b_out", [2, TOK, 520], BF16, kind="Internal").ap()
    cc1f_in = nc_.dram_tensor("cc1f_in", [512, TOK], F32, kind="Internal").ap()
    cc1f_out = nc_.dram_tensor("cc1f_out", [2, 512, TOK], F32, kind="Internal").ap()
    knb_h = nc_.dram_tensor("knb_h", [4, T * 33], BF16, kind="Internal").ap()
    qb_h = nc_.dram_tensor("qb_h", [4, T * 32], BF16, kind="Internal").ap()

    WJOBS = []
    for src, dst in ((wg_i, qg_o), (wu_i, qu_o)):
        for r0, rr in ((0, 128), (128, 128), (256, 96)):
            WJOBS.append(("q", src, r0, rr, 0, 1024, dst))
    for c0, w in ((0, 1024), (1024, 1024), (2048, 768)):
        WJOBS.append(("q", wd_i, 0, 128, c0, w, qd_o))
    for r0 in range(0, 512, 128):
        WJOBS.append(("b", wfc_i, r0, 128, 0, 1024, efc_o))
    NWJ = len(WJOBS)
    V_INIT = 1
    V_WJ = lambda j: 2 + j
    V_PJ1 = lambda k: 2 + NWJ + k
    V_PJSC = 2 + NWJ + 4
    V_PJ2 = lambda k: V_PJSC + 1 + k
    V_X0 = V_PJ2(3)
    V_XS = lambda i: V_X0 + 2 * i + 1
    V_X = lambda i: V_X0 + 2 * i + 2
    V_K0 = V_X(NTILE - 1)
    V_K2 = lambda i: V_K0 + 2 * i + 1
    V_KQ = lambda i: V_K0 + 2 * i + 2
    V_SCAN0 = V_KQ(NTILE - 1)
    V_SQF = V_SCAN0 + T + 1
    V_END = V_SQF + 2
    S_X = lambda i: i + 1
    S_XP0 = NTILE
    S_XP = lambda n: S_XP0 + n + 1
    S_XPALL = S_XP0 + 32
    S_KQ = lambda i: S_XPALL + i + 1
    S_VB0 = S_XPALL + NTILE
    S_VB = lambda n: S_VB0 + n + 1
    S_VBALL = S_VB0 + 16
    S_SCAN0 = S_VBALL
    T_XP = lambda n: n + 1
    T_XPALL = 32
    T_KQ = lambda i: T_XPALL + i + 1
    T_KQALL = T_XPALL + NTILE
    T_VB = lambda n: T_KQALL + n + 1
    L_PRE = 2
    L_WJ = lambda j: L_PRE + j + 1
    L_PJ1 = lambda k: L_PRE + NWJ + k + 1
    L_PJ2 = lambda k: L_PRE + NWJ + 4 + k + 1
    L_X = lambda i: L_PRE + NWJ + 8 + i + 1
    L_CH0 = L_PRE + NWJ + 8 + NTILE
    L_CH = lambda c: L_CH0 + 8 * (c + 1)
    ST_WJ = lambda j: j + 1
    ST_PJ = lambda k: NWJ + k + 1
    ST_XQT = NWJ + 4 + 8
    ST_ALL = ST_XQT + 1
    TD_TILE = lambda i: 16 * (i + 1)
    TD_ALL = 16 * NTILE
    BD_PACK = 2 * NTILE + 4
    BD_UNP = BD_PACK + 4 + 256
    NCAST = 16
    _cnt = {0: 0, 1: 0}
    LSLOT = {}
    for _j in range(NWJ):
        _cnt[_j % 2] += 1
        LSLOT[("wj", _j)] = (_j % 2, _cnt[_j % 2])
    for _k in range(4):
        _cnt[_k % 2] += 1
        LSLOT[("pj1", _k)] = (_k % 2, _cnt[_k % 2])
    for _k in range(4):
        _cnt[_k % 2] += 1
        LSLOT[("pj2", _k)] = (_k % 2, _cnt[_k % 2])
    for _i in range(NTILE):
        _cnt[_i % 2] += 1
        LSLOT[("x", _i)] = (_i % 2, _cnt[_i % 2])
    for _c in range(NCHK):
        _cnt[_c % 2] += 8
        LSLOT[("ch", _c)] = (_c % 2, _cnt[_c % 2])

    with contextlib.ExitStack() as es:
        sb = lambda n, sh, dt=F32: es.enter_context(nc_.sbuf_tensor(n, sh, dt))
        AR1 = sb("AR1", [128, 8192])
        AR2 = sb("AR2", [128, 16384], BF16)
        AR3 = sb("AR3", [128, 16384], BF16)
        AR4 = sb("AR4", [128, 8192], BF16)
        AR5 = sb("AR5", [128, 8192], BF16)
        AR6 = sb("AR6", [128, 8192])
        kn_tm = sb("kn_tm", [128, NTILE * 264], BF16)
        q_tm = sb("q_tm", [128, NTILE * 256], BF16)
        NV = sb("NV", [128, T])
        sgr = sb("sgr", [1, 1], I32)
        bbl = sb("bbl", [128, 2])
        st1 = sb("st1", [128, 1]); st2 = sb("st2", [128, 1])
        mu_ = sb("mu_", [128, 1]); vv_ = sb("vv_", [128, 1])
        m2_ = sb("m2_", [128, 1]); var_ = sb("var_", [128, 1])
        rsd_ = sb("rsd_", [128, 1]); seps = sb("seps", [128, 1])
        k2 = sb("k2", [128, 256]); k2s = sb("k2s", [128, 8])
        kbuf = sb("kbuf", [128, 256])
        nrm = sb("nrm", [128, 8]); negn = sb("negn", [128, 8]); rn = sb("rn", [128, 8])
        sS = sb("sS", [128, 66]); sbS = sb("sbS", [128, 32])
        scr33 = sb("scr33", [128, 33]); scr2 = sb("scr2", [128, 32])
        sdl = sb("sdl", [128, 1])
        bsm = sb("bsm", [128, 4]); bst = sb("bst", [128, 1])
        bsc = sb("bsc", [128, 1]); brs = sb("brs", [128, 1])
        wdb = AR1[:, 0:2048]
        qsA = AR1[:, 2048:3072]; qsB = AR1[:, 3072:4096]
        qsC = AR1[:, 4096:5120]; qsD = AR1[:, 5120:6144]
        qsE = AR1[:, 6144:7168]
        qam = AR1[:, 7168:7232]; qsc_ = AR1[:, 7232:7296]
        wout = AR1[:, 7296:7808].bitcast(BF16)

        def qsw(w):
            return {"A": qsA[:, 0:w], "B": qsB[:, 0:w], "C": qsC[:, 0:w],
                    "D": qsD[:, 0:w], "E": qsE[:, 0:w],
                    "am": qam[:, 0:w // 16], "sc": qsc_[:, 0:w // 16]}
        ssq = AR1[:, 0:4096]
        sil = AR1[:, 4096:6144]
        xnT = AR2
        skn = AR2[:, 0:2 * CH * 33]
        xqT = AR3
        nvS = AR3.bitcast(F32)[:, 0:4096]
        btS = AR3.bitcast(F32)[:, 4096:8192]
        sqb = AR3[:, 0:2 * CH * 32]
        winL = AR4
        BT = AR4.bitcast(F32)
        kvqbL = AR5
        sov = AR5.bitcast(F32)
        xdb = AR6[:, 0:2048]
        xnf = AR6[:, 2048:3072]
        lns = AR6[:, 3072:4096]
        xnb = AR6[:, 6144:7168].bitcast(BF16)
        xqb = AR6[:, 7168:8192].bitcast(BF16)
        xpT = AR6.bitcast(BF16)
        qs_x = {"A": lns, "B": qsB, "C": qsC, "D": qsD, "E": qsE,
                "am": qam, "sc": qsc_}
        pb = [es.enter_context(nc_.psum_tensor(f"pb{k}", [128, 512], F32))
              for k in range(8)]
        LD = es.enter_context(nc_.semaphore("LD"))
        LDA = es.enter_context(nc_.semaphore("LDA"))
        LDB = es.enter_context(nc_.semaphore("LDB"))
        LDP = es.enter_context(nc_.semaphore("LDP"))
        TDA = es.enter_context(nc_.semaphore("TDA"))
        TDB = es.enter_context(nc_.semaphore("TDB"))
        CLD = es.enter_context(nc_.semaphore("CLD"))
        TD = es.enter_context(nc_.semaphore("TD"))
        BD = es.enter_context(nc_.semaphore("BD"))
        ST = es.enter_context(nc_.semaphore("ST"))
        VS = es.enter_context(nc_.semaphore("VS"))
        SS = es.enter_context(nc_.semaphore("SS"))
        TS = es.enter_context(nc_.semaphore("TS"))
        PP = es.enter_context(nc_.semaphore("PP"))
        CC = es.enter_context(nc_.semaphore("CC"))
        block = es.enter_context(nc_.Block())

        @block.sync
        def _(sync):
            sync.dma_start(out=sgr[:], in_=gsel_i).then_inc(LDP, 16)
            sync.dma_start(out=bbl[:], in_=bbeta_i).then_inc(LDP, 16)
            def w_store(j):
                kind, src, r0, rr, c0, w, dst = WJOBS[j]
                sync.wait_ge(VS, V_WJ(j))
                sync.dma_start(out=dst[r0:r0 + rr, c0:c0 + w],
                               in_=wout[0:rr, 0:w]).then_inc(ST, 16)

            for j, (kind, src, r0, rr, c0, w, dst) in enumerate(WJOBS):
                if j >= 2:
                    w_store(j - 2)
                sync.dma_start(out=wdb[0:rr, (j % 2) * 1024:(j % 2) * 1024 + w],
                               in_=src[r0:r0 + rr, c0:c0 + w]).then_inc(
                                   LDA if j % 2 == 0 else LDB, 16)
            w_store(NWJ - 2)
            for k in range(4):
                if k >= 2:
                    sync.wait_ge(VS, V_PJ1(k - 2))
                else:
                    sync.wait_ge(VS, V_WJ(NWJ - 1 - k))
                sync.dma_start(out=wdb[:, (k % 2) * 1024:(k % 2 + 1) * 1024],
                               in_=wpj_i[:, k * 1024:(k + 1) * 1024]).then_inc(
                                   LDA if k % 2 == 0 else LDB, 16)
            w_store(NWJ - 1)
            for k in range(4):
                if k >= 2:
                    sync.wait_ge(VS, V_PJ2(k - 2))
                    sync.dma_start(out=epj_o[:, (k - 2) * 1024:(k - 1) * 1024],
                                   in_=wout[:, 0:1024]).then_inc(ST, 16)
                else:
                    sync.wait_ge(VS, V_PJ1(2 + k))
                sync.dma_start(out=wdb[:, (k % 2) * 1024:(k % 2 + 1) * 1024],
                               in_=wpj_i[:, k * 1024:(k + 1) * 1024]).then_inc(
                                   LDA if k % 2 == 0 else LDB, 16)
            for k in range(2, 4):
                sync.wait_ge(VS, V_PJ2(k))
                sync.dma_start(out=epj_o[:, k * 1024:(k + 1) * 1024],
                               in_=wout[:, 0:1024]).then_inc(ST, 16)
            def x_tp(i):
                sync.wait_ge(VS, V_X(i))
                sl = (i % 2) * 1024
                TDX = TDA if i % 2 == 0 else TDB
                for d8 in range(8):
                    sync.dma_start_transpose(
                        out=xnT[:, d8 * 2048 + i * 128: d8 * 2048 + (i + 1) * 128],
                        in_=xnb[:, sl + d8 * 128: sl + (d8 + 1) * 128]).then_inc(TDX, 16)
                    sync.dma_start_transpose(
                        out=xqT[:, d8 * 2048 + i * 128: d8 * 2048 + (i + 1) * 128],
                        in_=xqb[:, sl + d8 * 128: sl + (d8 + 1) * 128]).then_inc(TDX, 16)

            for i in range(NTILE):
                if i >= 2:
                    x_tp(i - 2)
                sync.dma_start(out=xdb[:, (i % 2) * 1024:(i % 2 + 1) * 1024],
                               in_=x_i[i * 128:(i + 1) * 128, :]).then_inc(
                                   LDA if i % 2 == 0 else LDB, 16)
            x_tp(NTILE - 2)
            x_tp(NTILE - 1)
            sync.wait_ge(TDA, 16 * 16 * (NTILE // 2))
            sync.wait_ge(TDB, 16 * 16 * (NTILE // 2))
            for d8 in range(8):
                sync.dma_start(out=xnT_o[d8 * 128:(d8 + 1) * 128, :],
                               in_=xnT[:, d8 * 2048:(d8 + 1) * 2048]).then_inc(TD, 16)
            for d8 in range(8):
                sync.dma_start(out=xqT_o[d8 * 128:(d8 + 1) * 128, :],
                               in_=xqT[:, d8 * 2048:(d8 + 1) * 2048]).then_inc(ST, 16)
            for i in range(NTILE):
                sync.wait_ge(VS, V_KQ(i))
                sync.dma_start(out=cc1b_in[i * 128:(i + 1) * 128, 0:264],
                               in_=kn_tm[:, i * 264:(i + 1) * 264]).then_inc(BD, 16)
                sync.dma_start(out=cc1b_in[i * 128:(i + 1) * 128, 264:520],
                               in_=q_tm[:, i * 256:(i + 1) * 256]).then_inc(BD, 16)
            sync.wait_ge(SS, S_VBALL)
            sync.dma_start(out=cc1f_in[0:128, :], in_=nvS[:, 0:2048]).then_inc(BD, 16)
            sync.dma_start(out=cc1f_in[128:256, :], in_=nvS[:, 2048:4096]).then_inc(BD, 16)
            sync.dma_start(out=cc1f_in[256:384, :], in_=btS[:, 0:2048]).then_inc(BD, 16)
            sync.dma_start(out=cc1f_in[384:512, :], in_=btS[:, 2048:4096]).then_inc(BD, 16)
            sync.wait_ge(CC, 2)
            reg = sync.alloc_register()
            sync.load(reg, sgr[0:1, 0:1])

            def unpack(g):
                for h in range(2):
                    sync.dma_start(out=NV[:, h * 2048:(h + 1) * 2048],
                                   in_=cc1f_out[h, g * 128:(g + 1) * 128, :]).then_inc(BD, 16)
                    sync.dma_start(out=BT[:, h * 2048:(h + 1) * 2048],
                                   in_=cc1f_out[h, 256 + g * 128:256 + (g + 1) * 128, :]).then_inc(BD, 16)
                for p in range(4):
                    head = 4 * g + p
                    for gi in range(32):
                        h, il = divmod(gi, 16)
                        sync.dma_start(
                            out=knb_h[p, gi * 128 * 33:(gi + 1) * 128 * 33]
                            .rearrange("(t c) -> t c", c=33),
                            in_=cc1b_out[h, il * 128:(il + 1) * 128,
                                         head * 33:(head + 1) * 33]).then_inc(BD, 16)
                        sync.dma_start(
                            out=qb_h[p, gi * 128 * 32:(gi + 1) * 128 * 32]
                            .rearrange("(t c) -> t c", c=32),
                            in_=cc1b_out[h, il * 128:(il + 1) * 128,
                                         264 + head * 32:264 + (head + 1) * 32]).then_inc(BD, 16)

            with sync.If_eq(reg, 0):
                unpack(0)
            with sync.Else():
                unpack(1)
            for c in range(NCHK):
                if c >= 2:
                    sync.wait_ge(VS, V_SCAN0 + (c - 1) * CH + 2)
                sl = (c % 2)
                LDX = LDA if c % 2 == 0 else LDB
                for p in range(4):
                    sync.dma_start(out=skn[32 * p:32 * p + 32, sl * CH * 33:(sl + 1) * CH * 33],
                                   in_=knb_h[p:p + 1, c * CH * 33:(c + 1) * CH * 33]
                                   .broadcast_to([32, CH * 33])).then_inc(LDX, 16)
                    sync.dma_start(out=sqb[32 * p:32 * p + 32, sl * CH * 32:(sl + 1) * CH * 32],
                                   in_=qb_h[p:p + 1, c * CH * 32:(c + 1) * CH * 32]
                                   .broadcast_to([32, CH * 32])).then_inc(LDX, 16)
            sync.wait_ge(VS, V_END)
            sync.dma_start(out=sov_o, in_=sov[:, 0:4096]).then_inc(ST, 16)
            sync.wait_ge(ST, 16 * ST_ALL)

        @block.gpsimd
        def _(g):
            for kc in range(8):
                g.dma_start(out=winL[:, kc * 1024:(kc + 1) * 1024],
                            in_=winT_i[kc * 128:(kc + 1) * 128, :]).then_inc(CLD, 16)
            for kc in range(8):
                g.dma_start(out=kvqbL[:, kc * 1024:(kc + 1) * 1024],
                            in_=kvqbT_i[kc * 128:(kc + 1) * 128, :]).then_inc(CLD, 16)
            g.wait_ge(BD, 16 * BD_PACK)
            g.collective_compute("AllGather", Op.bypass, replica_groups=GROUPS,
                                 ins=[cc1b_in], outs=[cc1b_out[0:2]]).then_inc(CC, 1)
            g.collective_compute("AllGather", Op.bypass, replica_groups=GROUPS,
                                 ins=[cc1f_in], outs=[cc1f_out[0:2]]).then_inc(CC, 1)
            g.wait_ge(BD, 16 * BD_UNP)
            g.wait_ge(VS, V_INIT)
            g.tensor_copy(sS[:, 32:33], NV[:, 0:1]).then_inc(PP, 1)
            for t_ in range(T):
                g.wait_ge(SS, S_SCAN0 + t_)
                cur = sS[:, (t_ % 2) * 33:(t_ % 2) * 33 + 32]
                nxt33 = sS[:, ((t_ + 1) % 2) * 33 + 32:((t_ + 1) % 2) * 33 + 33]
                g.tensor_scalar(sbS[:], cur, BT[:, t_:t_ + 1], None, Op.mult).then_inc(PP, 1)
                if t_ + 1 < T:
                    g.tensor_copy(nxt33, NV[:, t_ + 1:t_ + 2]).then_inc(PP, 1)
                else:
                    g.sem_inc(PP, 1)

        @block.tensor
        def _(t):
            t.wait_ge(CLD, 16 * NCAST)
            t.wait_ge(TDA, 16 * 16 * (NTILE // 2))
            t.wait_ge(TDB, 16 * 16 * (NTILE // 2))
            n = 0
            for j in range(8):
                for f in range(4):
                    ps = pb[n % 4]
                    if n >= 4:
                        t.wait_ge(SS, S_XP(n - 4))
                    for kc in range(8):
                        mi = t.matmul(ps[:, 0:512],
                                      winL[:, kc * 1024 + j * 128: kc * 1024 + (j + 1) * 128],
                                      xnT[:, kc * 2048 + f * 512: kc * 2048 + (f + 1) * 512],
                                      start=(kc == 0), stop=(kc == 7))
                    mi.then_inc(TS, 1)
                    n += 1
            t.wait_ge(SS, S_XPALL)
            for i in range(NTILE):
                ps = pb[4 + (i % 2)]
                if i >= 2:
                    t.wait_ge(VS, V_KQ(i - 2))
                for kc in range(8):
                    mi = t.matmul(ps[:, 0:512],
                                  xpT[:, kc * 2048 + i * 128: kc * 2048 + (i + 1) * 128],
                                  kvqbL[:, kc * 1024: kc * 1024 + 512],
                                  start=(kc == 0), stop=(kc == 7))
                mi.then_inc(TS, 1)
            n = 0
            for m in range(4):
                for f in range(4):
                    ps = pb[6 + (n % 2)]
                    if n >= 2:
                        t.wait_ge(SS, S_VB(n - 2))
                    for kc in range(8):
                        mi = t.matmul(ps[:, 0:512],
                                      kvqbL[:, kc * 1024 + 512 + m * 128: kc * 1024 + 512 + (m + 1) * 128],
                                      xpT[:, kc * 2048 + f * 512: kc * 2048 + (f + 1) * 512],
                                      start=(kc == 0), stop=(kc == 7))
                    mi.then_inc(TS, 1)
                    n += 1

        @block.vector
        def _(v):
            v.memset(seps[:], 1e-5)
            v.memset(kn_tm.rearrange("p (n c) -> p n c", c=33)[:, :, 32:33], 1.0)
            v.memset(sS[:], 0.0)
            v.drain().then_inc(VS, 1)
            for j, (kind, src, r0, rr, c0, w, dst) in enumerate(WJOBS):
                sl_, cnt_ = LSLOT[("wj", j)]
                v.wait_ge(LDA if sl_ == 0 else LDB, 16 * cnt_)
                if j >= 1:
                    v.wait_ge(ST, 16 * ST_WJ(j - 1))
                win_ = wdb[:, (j % 2) * 1024:(j % 2) * 1024 + w]
                if kind == "q":
                    emit_quant(v, win_, wout[:, 0:w], qsw(w), w)
                else:
                    v.reduce_sum(bsm[:, 0:1], win_, axis=AX, apply_absolute_value=True)
                    v.drain()
                    v.tensor_scalar(bsc[:], bsm[:, 0:1], 1.0 / w, 1e-5, Op.mult, Op.max)
                    v.drain()
                    v.reciprocal(brs[:], bsc[:])
                    v.drain()
                    v.tensor_scalar(qsB[:, 0:w].bitcast(I32), win_, brs[:, 0:1],
                                    None, Op.mult)
                    v.drain()
                    v.tensor_scalar(qsC[:, 0:w], qsB[:, 0:w].bitcast(I32),
                                    1.0, -1.0, Op.min, Op.max)
                    v.drain()
                    v.tensor_scalar(wout[:, 0:w], qsC[:, 0:w], bsc[:, 0:1],
                                    None, Op.mult)
                v.drain().then_inc(VS, 1)
            for k in range(4):
                sl_, cnt_ = LSLOT[("pj1", k)]
                v.wait_ge(LDA if sl_ == 0 else LDB, 16 * cnt_)
                v.reduce_sum(bsm[:, k:k + 1], wdb[:, (k % 2) * 1024:(k % 2 + 1) * 1024],
                             axis=AX, apply_absolute_value=True)
                v.drain().then_inc(VS, 1)
            v.reduce_sum(bst[:], bsm[:], axis=AX)
            v.drain()
            v.tensor_scalar(bsc[:], bst[:], 1.0 / HID1, 1e-5, Op.mult, Op.max)
            v.drain()
            v.reciprocal(brs[:], bsc[:])
            v.drain().then_inc(VS, 1)
            for k in range(4):
                sl_, cnt_ = LSLOT[("pj2", k)]
                v.wait_ge(LDA if sl_ == 0 else LDB, 16 * cnt_)
                v.wait_ge(ST, 16 * ST_PJ(k - 1) if k >= 1 else 16 * ST_WJ(NWJ - 1))
                win_ = wdb[:, (k % 2) * 1024:(k % 2 + 1) * 1024]
                v.tensor_scalar(qsB[:, 0:1024].bitcast(I32), win_, brs[:, 0:1],
                                None, Op.mult)
                v.drain()
                v.tensor_scalar(qsC[:, 0:1024], qsB[:, 0:1024].bitcast(I32),
                                1.0, -1.0, Op.min, Op.max)
                v.drain()
                v.tensor_scalar(wout[:, 0:1024], qsC[:, 0:1024], bsc[:, 0:1],
                                None, Op.mult)
                v.drain().then_inc(VS, 1)
            for i in range(NTILE):
                sl_, cnt_ = LSLOT[("x", i)]
                v.wait_ge(LDA if sl_ == 0 else LDB, 16 * cnt_)
                if i >= 2:
                    v.wait_ge(TDA if (i - 2) % 2 == 0 else TDB,
                              16 * 16 * ((i - 2) // 2 + 1))
                xt = xdb[:, (i % 2) * 1024:(i % 2 + 1) * 1024]
                v.reduce_sum(st1[:], xt, axis=AX)
                v.scalar_tensor_tensor(lns[:], xt, 1.0, xt, Op.mult, Op.mult,
                                       accum_out=st2[:])
                v.drain()
                v.tensor_scalar(mu_[:], st1[:], 1.0 / D, None, Op.mult)
                v.tensor_scalar(vv_[:], st2[:], 1.0 / D, None, Op.mult)
                v.tensor_tensor(m2_[:], mu_[:], mu_[:], Op.mult)
                v.nop(cycle_cnt=64)
                v.tensor_tensor(var_[:], vv_[:], m2_[:], Op.subtract)
                v.drain().then_inc(VS, 1)
                v.wait_ge(SS, S_X(i))
                v.reciprocal(rsd_[:], var_[:])
                v.nop(cycle_cnt=64)
                v.tensor_scalar(xnf[:], xt, mu_[:], rsd_[:], Op.subtract, Op.mult)
                v.nop(cycle_cnt=64)
                sl = (i % 2) * 1024
                v.tensor_copy(xnb[:, sl - 6144 + 6144:sl + 1024] if False else xnb[:, (i % 2) * 1024:(i % 2 + 1) * 1024], xnf[:])
                emit_quant(v, xnf[:], xqb[:, (i % 2) * 1024:(i % 2 + 1) * 1024], qs_x, D)
                v.drain().then_inc(VS, 1)
            for i in range(NTILE):
                v.wait_ge(TS, T_KQ(i))
                pk = pb[4 + (i % 2)]
                v.tensor_copy(kbuf[:], pk[:, 0:256])
                v.tensor_copy(q_tm[:, i * 256:(i + 1) * 256], pk[:, 256:512])
                v.tensor_tensor(k2[:], kbuf[:], kbuf[:], Op.mult)
                v.nop(cycle_cnt=64)
                v.reduce_sum(k2s[:], k2[:].rearrange("p (g j) -> p g j", j=NJ), axis=AX)
                v.drain().then_inc(VS, 1)
                v.wait_ge(SS, S_KQ(i))
                v.tensor_scalar(negn[:], nrm[:], 1e-6, -1.0, Op.add, Op.mult)
                v.nop(cycle_cnt=64)
                v.reciprocal(rn[:], negn[:])
                v.nop(cycle_cnt=64)
                v.tensor_tensor(
                    kn_tm[:, i * 264:(i + 1) * 264].rearrange("p (g c) -> p g c", c=33)[:, :, 0:32],
                    kbuf[:].rearrange("p (g j) -> p g j", j=NJ),
                    rn[:].unsqueeze(2).broadcast_to([128, 8, NJ]), Op.mult)
                v.drain().then_inc(VS, 1)
            v.wait_ge(BD, 16 * BD_UNP)
            for t_ in range(T):
                c, r = divmod(t_, CH)
                sl = c % 2
                if r == 0:
                    sl_, cnt_ = LSLOT[("ch", c)]
                    v.wait_ge(LDA if sl_ == 0 else LDB, 16 * cnt_)
                cur = sS[:, (t_ % 2) * 33:(t_ % 2) * 33 + 33]
                kn33 = skn[:, sl * CH * 33 + r * 33: sl * CH * 33 + (r + 1) * 33]
                kn32 = skn[:, sl * CH * 33 + r * 33: sl * CH * 33 + r * 33 + 32]
                v.wait_ge(SS, S_SCAN0 + t_)
                v.wait_ge(PP, max(1, 2 * t_ + 1))
                v.scalar_tensor_tensor(scr33[:], cur, 1.0, kn33, Op.mult, Op.mult,
                                       accum_out=sdl[:, 0:1])
                if t_ > 0:
                    pc, pr = divmod(t_ - 1, CH)
                    psl = pc % 2
                    v.scalar_tensor_tensor(scr2[:], cur[:, 0:32], 1.0,
                                           sqb[:, psl * CH * 32 + pr * 32: psl * CH * 32 + (pr + 1) * 32],
                                           Op.mult, Op.mult, accum_out=ssq[:, t_ - 1:t_])
                else:
                    v.nop(cycle_cnt=64)
                v.wait_ge(PP, 2 * t_ + 2)
                v.scalar_tensor_tensor(scr2[:], kn32, sdl[:, 0:1], sbS[:],
                                       Op.mult, Op.subtract).then_inc(VS, 1)
            v.wait_ge(SS, S_SCAN0 + T)
            lc, lr = divmod(T - 1, CH)
            lsl = lc % 2
            v.scalar_tensor_tensor(scr2[:], sS[:, (T % 2) * 33:(T % 2) * 33 + 32], 1.0,
                                   sqb[:, lsl * CH * 32 + lr * 32: lsl * CH * 32 + (lr + 1) * 32],
                                   Op.mult, Op.mult, accum_out=ssq[:, T - 1:T])
            v.drain().then_inc(VS, 1)
            for hf in range(2):
                v.wait_ge(SS, S_SCAN0 + T + 1 + hf)
                v.tensor_tensor(sov[:, hf * 2048:(hf + 1) * 2048],
                                ssq[:, hf * 2048:(hf + 1) * 2048], sil[:], Op.mult)
                v.drain().then_inc(VS, 1)

        @block.scalar
        def _(s):
            for i in range(NTILE):
                s.wait_ge(VS, V_XS(i))
                s.activation(var_[:], var_[:], AF.Sqrt, bias=seps[:]).then_inc(SS, 1)
            n = 0
            for j in range(8):
                for f in range(4):
                    ps = pb[n % 4]
                    s.wait_ge(TS, T_XP(n))
                    s.activation(xpT[:, j * 2048 + f * 512: j * 2048 + (f + 1) * 512],
                                 ps[:, 0:512], AF.Silu).then_inc(SS, 1)
                    n += 1
            for i in range(NTILE):
                s.wait_ge(VS, V_K2(i))
                s.activation(nrm[:], k2s[:], AF.Sqrt).then_inc(SS, 1)
            s.wait_ge(ST, 16 * ST_XQT)
            s.wait_ge(LDP, 32)
            n = 0
            for m in range(4):
                for f in range(4):
                    ps = pb[6 + (n % 2)]
                    s.wait_ge(TS, T_VB(n))
                    if m < 2:
                        mi = s.copy(nvS[:, m * 2048 + f * 512: m * 2048 + (f + 1) * 512],
                                    ps[:, 0:512])
                    else:
                        mi = s.activation(btS[:, (m - 2) * 2048 + f * 512:(m - 2) * 2048 + (f + 1) * 512],
                                          ps[:, 0:512], AF.Sigmoid,
                                          bias=bbl[:, m - 2:m - 1])
                    mi.then_inc(SS, 1)
                    n += 1
            for t_ in range(T):
                s.wait_ge(VS, V_SCAN0 + t_ + 1)
                nxt = sS[:, ((t_ + 1) % 2) * 33:((t_ + 1) % 2) * 33 + 32]
                s.activation(nxt, scr2[:], AF.Tanh, scale=-1.0).then_inc(SS, 1)
            for hf in range(2):
                s.wait_ge(VS, V_SQF + hf)
                s.activation(sil[:], ssq[:, hf * 2048:(hf + 1) * 2048],
                             AF.Silu).then_inc(SS, 1)
    return nc_



def _f32c(a):
    return np.ascontiguousarray(np.asarray(a, np.float32))


def _bfc(a):
    return np.ascontiguousarray(np.asarray(a))


def kernel(**inp):
    f = {k: np.asarray(v) for k, v in inp.items()}
    x = _f32c(f["L0_attn_x"])

    def tok_slice(a, c):
        b_, th = c // 2, c % 2
        return a[b_, th * TOK:(th + 1) * TOK]

    ncM = build_mega()
    kvqbT = np.concatenate([f["L0_attn_W_k"].T, f["L0_attn_W_q"].T,
                            f["L0_attn_W_v"].T, f["L0_attn_W_beta"].T], axis=1)
    kvqbT = _f32c(kvqbT)
    winT = _f32c(f["L0_attn_W_in"].T)
    bbeta = _f32c(f["L0_attn_b_beta"].reshape(2, 128).T)
    maps = []
    for c in CORES:
        b_, g_ = c // 2, c % 2
        maps.append(dict(
            x=_f32c(tok_slice(x, c)),
            winT=winT, kvqbT=kvqbT, bbeta=bbeta,
            gsel=np.array([[g_]], np.int32),
            wg=_f32c(f["L0_ffn_w_gate"][c * 352:(c + 1) * 352]),
            wu=_f32c(f["L0_ffn_w_up"][c * 352:(c + 1) * 352]),
            wd=_f32c(f["L0_ffn_w_down"][c * 128:(c + 1) * 128]),
            wfc=_f32c(f["L1_ffn_w_fc"][c * 512:(c + 1) * 512]),
            wpj=_f32c(f["L1_ffn_w_proj"][c * 128:(c + 1) * 128]),
        ))
    rM = run_bass_kernel_spmd(ncM, maps, core_ids=CORES).results
    qg = np.concatenate([np.asarray(r["qg_o"]) for r in rM], 0)
    qu = np.concatenate([np.asarray(r["qu_o"]) for r in rM], 0)
    qd = np.concatenate([np.asarray(r["qd_o"]) for r in rM], 0)
    efc = np.concatenate([np.asarray(r["efc_o"]) for r in rM], 0)
    epj = np.concatenate([np.asarray(r["epj_o"]) for r in rM], 0)
    outv = np.empty((B, T, 256), np.float32)
    for c in CORES:
        b_, g_ = c // 2, c % 2
        sov = np.asarray(rM[c]["sov_o"])
        outv[b_, :, 128 * g_:128 * (g_ + 1)] = sov.T

    def gather_tok(res, key, width, dt_=np.float32):
        out = np.empty((B, T, width), dt_)
        for c in CORES:
            b_, th = c // 2, c % 2
            out[b_, th * TOK:(th + 1) * TOK] = np.asarray(res[c][key])
        return out

    ncA2 = build_A2()
    maps = [dict(xqT=_bfc(np.asarray(rM[c]["xqT_o"])),
                 wqgT=_bfc(qg.T), wquT=_bfc(qu.T)) for c in CORES]
    rA2 = run_bass_kernel_spmd(ncA2, maps, core_ids=CORES).results
    hq = gather_tok(rA2, "hq_o", HID0, BFD)

    ncA3 = build_dn("A3", HID0, D)
    maps = [dict(inT=_bfc(tok_slice(hq, c).T), wT=_bfc(qd.T)) for c in CORES]
    rA3 = run_bass_kernel_spmd(ncA3, maps, core_ids=CORES).results
    g0 = gather_tok(rA3, "o", D)

    ncC1 = build_C1()
    maps = [dict(x=_f32c(tok_slice(x, c)),
                 g0=_f32c(tok_slice(g0, c)),
                 ovT=_f32c(tok_slice(outv, c).T),
                 w_outT=_f32c(f["L0_attn_W_out"].T)) for c in CORES]
    rC1 = run_bass_kernel_spmd(ncC1, maps, core_ids=CORES).results
    x1 = gather_tok(rC1, "x1_o", D)
    xn1 = gather_tok(rC1, "xn1_o", D, BFD)

    ncC2 = build_C2()
    maps = [dict(xn1T=_bfc(np.ascontiguousarray(tok_slice(xn1, c).T)),
                 bwT=_f32c(f["L1_attn_B_w"].T), bb=_f32c(f["L1_attn_B_b"].reshape(1, 256)),
                 dwT=_f32c(f["L1_attn_D_w"].T), db=_f32c(f["L1_attn_D_b"].reshape(1, D)),
                 vT=_f32c(f["L1_ffn_v"].T)) for c in CORES]
    rC2 = run_bass_kernel_spmd(ncC2, maps, core_ids=CORES).results
    bu = gather_tok(rC2, "bu_o", 256)
    du = gather_tok(rC2, "du_o", D)
    t1 = gather_tok(rC2, "t1_o", 16)

    ncC3 = build_C3()
    maps = [dict(xn1T=_bfc(np.ascontiguousarray(tok_slice(xn1, c).T)),
                 t1T=_f32c(np.ascontiguousarray(tok_slice(t1, c).T)),
                 fbT=_f32c(f["L1_ffn_fast_b"].T),
                 uT=_f32c(f["L1_ffn_u"].T),
                 wfcT=_bfc(efc.T),
                 gate=_f32c(np.asarray(f["L1_ffn_gate"]).reshape(1, 1))) for c in CORES]
    rC3 = run_bass_kernel_spmd(ncC3, maps, core_ids=CORES).results
    r2 = gather_tok(rC3, "r2_o", HID1, BFD)

    ncC4 = build_dn("C4", HID1, D)
    maps = [dict(inT=_bfc(np.ascontiguousarray(tok_slice(r2, c).T)), wT=_bfc(epj.T)) for c in CORES]
    rC4 = run_bass_kernel_spmd(ncC4, maps, core_ids=CORES).results
    g1 = gather_tok(rC4, "o", D)

    ncD = build_D()
    maps = []
    for c in CORES:
        b_, nh = c // 2, c % 2
        maps.append(dict(BU=_f32c(np.ascontiguousarray(bu[b_, :, nh * 128:(nh + 1) * 128].T)),
                         Av=_f32c(f["L1_attn_A"][nh * 128:(nh + 1) * 128].reshape(128, 1))))
    rD = run_bass_kernel_spmd(ncD, maps, core_ids=CORES).results
    states = np.empty((B, T, 256), np.float32)
    for c in CORES:
        b_, nh = c // 2, c % 2
        states[b_, :, nh * 128:(nh + 1) * 128] = np.asarray(rD[c]["SO"]).T

    ncE = build_E()
    maps = [dict(x1=_f32c(tok_slice(x1, c)), du=_f32c(tok_slice(du, c)),
                 g1=_f32c(tok_slice(g1, c)),
                 stT=_f32c(np.ascontiguousarray(tok_slice(states, c).T)),
                 cwT=_f32c(f["L1_attn_C_w"].T),
                 cb=_f32c(f["L1_attn_C_b"].reshape(1, D))) for c in CORES]
    rE = run_bass_kernel_spmd(ncE, maps, core_ids=CORES).results
    out = gather_tok(rE, "o", D)
    return out.astype(np.float32)
